# version 23
# baseline (speedup 1.0000x reference)
import numpy as np
import ml_dtypes
import concourse.bass as bass
import concourse.mybir as mybir
from concourse.bass_utils import run_bass_kernel_spmd

F32 = mybir.dt.float32
BF16 = mybir.dt.bfloat16
AF = mybir.ActivationFunctionType
ALU = mybir.AluOpType

T, D, H, DH = 2048, 1024, 16, 64
C = 128
NCH = T // C
NDC = D // 128
GAMMAS = np.array([1.0 - 2.0 ** (-5.0 - i) for i in range(H)], dtype=np.float64)
N_CONST_DMAS = 22
KB_NCH = 16
KB_PH3 = True
KB_MOD = 12
KB_STAGE = 99
KB_SH = 16
KB_ST7 = 3
SLICE_ORDER = [0, 1, 2, 3, 4, 5, 10, 11, 6, 7, 8, 9]


def _consts():
    mask = np.zeros((128, 16 * 128), dtype=np.float32)
    blk = np.triu(np.ones((128, 128), dtype=np.float32)) * 0.125
    for h in range(16):
        mask[:, 128 * h:128 * (h + 1)] = blk
    lng = np.repeat(np.log(GAMMAS), DH).reshape(1, D).astype(np.float32)
    tvec = (np.arange(16)[None, :] * 128 + np.arange(128)[:, None]).astype(np.float32)
    ntvec = -tvec
    id2 = (np.arange(64)[None, :] == (np.arange(128)[:, None] % 64)).astype(np.float32)
    ones1 = np.ones((1, 128), dtype=np.float32)
    actb = np.zeros((128, 2), dtype=np.float32)
    actb[:, 0] = -np.log(8.0)
    actb[:, 1] = 1e-5
    halfind = np.zeros((128, 2), dtype=np.float32)
    halfind[:64, 0] = 1.0
    halfind[64:, 1] = 1.0
    return dict(
        mask=mask.astype(ml_dtypes.bfloat16),
        lng=lng, tvec=tvec, ntvec=ntvec,
        id2=id2.astype(ml_dtypes.bfloat16),
        ones1=ones1.astype(ml_dtypes.bfloat16),
        halfind=halfind, halfindT=np.ascontiguousarray(halfind.T), actb=actb,
    )


def build_nc():
    nc = bass.Bass("TRN2", target_bir_lowering=False)
    cst = _consts()

    x_d = nc.dram_tensor("x", [T, D], F32, kind="ExternalInput").ap()
    wq_d = nc.dram_tensor("wq", [D, D], F32, kind="ExternalInput").ap()
    wk_d = nc.dram_tensor("wk", [D, D], F32, kind="ExternalInput").ap()
    wv_d = nc.dram_tensor("wv", [D, D], F32, kind="ExternalInput").ap()
    wo_d = nc.dram_tensor("wo", [D, D], F32, kind="ExternalInput").ap()
    gnw_d = nc.dram_tensor("gnw", [128, 8], F32, kind="ExternalInput").ap()
    gnb_d = nc.dram_tensor("gnb", [128, 8], F32, kind="ExternalInput").ap()
    win_d = nc.dram_tensor("win", [D, 64], F32, kind="ExternalInput").ap()
    bin_d = nc.dram_tensor("binp", [64, 1], F32, kind="ExternalInput").ap()
    w1p_d = nc.dram_tensor("w1p", [128, 8], F32, kind="ExternalInput").ap()
    b1p_d = nc.dram_tensor("b1p", [128, 8], F32, kind="ExternalInput").ap()
    w2p_d = nc.dram_tensor("w2p", [128, 8], F32, kind="ExternalInput").ap()
    b2_d = nc.dram_tensor("b2", [64, 1], F32, kind="ExternalInput").ap()
    wout_d = nc.dram_tensor("wout", [64, D], F32, kind="ExternalInput").ap()
    bout_d = nc.dram_tensor("bout", [1, D], F32, kind="ExternalInput").ap()
    crow_d = nc.dram_tensor("crow", [128, 8], F32, kind="ExternalInput").ap()
    wmod_d = nc.dram_tensor("wmod", [D, 6 * D], F32, kind="ExternalInput").ap()
    out_d = nc.dram_tensor("out", [T, D], F32, kind="ExternalOutput").ap()

    wsc = [nc.dram_tensor(f"wsc{i}", [D, D], BF16, kind="Internal").ap()
           for i in range(4)]

    mask_d = nc.inline_tensor(cst["mask"], name="c_mask").ap()
    lng_d = nc.inline_tensor(cst["lng"], name="c_lng").ap()
    ntv_d = nc.inline_tensor(cst["ntvec"], name="c_ntvec").ap()
    tvec_d = nc.inline_tensor(cst["tvec"], name="c_tvec").ap()
    id2_d = nc.inline_tensor(cst["id2"], name="c_id2").ap()
    ones1_d = nc.inline_tensor(cst["ones1"], name="c_ones1").ap()
    hind_d = nc.inline_tensor(cst["halfind"], name="c_hind").ap()
    hindT_d = nc.inline_tensor(cst["halfindT"], name="c_hindT").ap()
    actb_d = nc.inline_tensor(cst["actb"], name="c_actb").ap()
    cbs_d = nc.dram_tensor("cbs", [2, 1024], BF16, kind="Internal").ap()

    A = nc.alloc_sbuf_tensor
    mask_sb = A("mask_sb", [128, 2048], BF16)
    lng_sb = A("lng_sb", [128, D], F32)
    ntv_sb = A("ntv_sb", [128, 16], F32)
    gascr = A("gascr", [128, 1024], BF16)
    tvec_sb = A("tvec_sb", [128, 16], F32)
    id2_sb = A("id2_sb", [128, 64], BF16)
    ones_sb = A("ones_sb", [1, 128], BF16)
    hind_sb = A("hind_sb", [128, 2], F32)
    hindT_sb = A("hindT_sb", [2, 128], F32)
    actb_sb = A("actb_sb", [128, 2], F32)
    gnwj_sb = A("gnwj_sb", [128, 8], F32)
    gnbj_sb = A("gnbj_sb", [128, 8], F32)
    wp1_sb = A("wp1_sb", [128, 8], F32)
    b1p_sb = A("b1p_sb", [128, 8], F32)
    w2p_sb = A("w2p_sb", [128, 8], F32)
    b1pp_sb = A("b1pp_sb", [128, 8], F32)
    bind_sb = A("bind_sb", [128, 1], F32)
    b2f_sb = A("b2f_sb", [64, 1], F32)
    b2b_sb = A("b2b_sb", [64, 1], BF16)
    cfm_sb = A("cfm_sb", [128, 8], F32)
    silf_sb = A("silf_sb", [128, 8], F32)
    silb_sb = A("silb_sb", [128, 8], BF16)
    bout_sb = A("bout_sb", [1, D], F32)
    bf_sb = A("bf_sb", [1, D], BF16)
    row2_sb = A("row2_sb", [1, D], BF16)
    mrow_sb = A("mrow_sb", [1, 512], BF16)
    cbT_sb = A("cbT_sb", [128, 8], F32)
    cbTb_sb = A("cbTb_sb", [128, 8], BF16)
    sbT_sb = A("sbT_sb", [128, 8], BF16)
    sbw_sb = A("sbw_sb", [128, 1], F32)

    wt = {n: A(f"wt_{n}", [128, NDC * D], BF16) for n in ("q", "k", "v", "o")}
    wmsl = A("wmsl", [128, NDC * 512], BF16)
    modt = A("modt", [128, 6 * D], BF16)
    windup_sb = A("windup", [128, NDC * 128], BF16)
    dg_sb = A("dg", [128, 8 * 64], BF16)
    wout_sb = A("wout_sb", [64, D], BF16)

    xa = [A(f"xa{i}", [128, D], F32) for i in range(2)]
    x2 = [A(f"x2_{i}", [128, D], F32) for i in range(2)]
    htl = [A(f"h{i}", [128, D], BF16) for i in range(2)]
    hT = [A(f"hT{i}", [128, D], BF16) for i in range(2)]
    tabq = [A(f"tabq{i}", [128, D], BF16) for i in range(2)]
    tabk = [A(f"tabk{i}", [128, D], BF16) for i in range(2)]
    qtm = [A(f"qtm{i}", [128, D], BF16) for i in range(2)]
    qfm = [A(f"qfm{i}", [128, D], BF16) for i in range(2)]
    ktm = [A(f"ktm{i}", [128, D], BF16) for i in range(2)]
    kfm = [A(f"kfm{i}", [128, D], BF16) for i in range(2)]
    vtm = [A(f"vtm{i}", [128, D], BF16) for i in range(2)]
    stl = [A(f"st{i}", [128, 2048], BF16) for i in range(2)]
    pbf = A("pbf", [128, 512], BF16)
    ybf = A("ybf", [128, NCH * D], BF16)
    y2s = A("y2s", [128, D], BF16)
    sqs = A("sqs", [128, D], BF16)
    gb1 = [A(f"gb1_{i}", [64, 128], BF16) for i in range(2)]

    xsum = A("xsum", [128, 16], F32)
    xsq = A("xsq", [128, 16], F32)
    mu = A("mu", [128, 16], F32)
    mu2 = A("mu2", [128, 16], F32)
    var = A("var", [128, 16], F32)
    rho = A("rho", [128, 16], F32)
    ysum_a = A("ysum_a", [128, 8], F32)
    ysq_a = A("ysq_a", [128, 8], F32)
    ysumc = A("ysumc", [128, 8], F32)
    ysqc = A("ysqc", [128, 8], F32)
    hstat = A("hstat", [128, 16], F32)
    hsc = A("hsc", [128, 32], F32)
    arep = A("arep", [128, 8], F32)
    brep = A("brep", [128, 8], F32)
    bbf = A("bbf", [128, 8], BF16)

    ps01 = nc.alloc_psum_tensor("ps01", [128, 1024], F32)
    psS2 = nc.alloc_psum_tensor("psS2", [128, 512], F32)
    psS3 = nc.alloc_psum_tensor("psS3", [128, 512], F32)
    ps4 = nc.alloc_psum_tensor("ps4", [128, 512], F32)
    psY = nc.alloc_psum_tensor("psY", [128, 1024], F32)
    ps7 = nc.alloc_psum_tensor("ps7", [128, 512], F32)

    sem_names = [
        "s_cst", "s_wc0", "s_wc1", "s_wc2", "s_wc3",
        "s_wtq", "s_wtk", "s_wtv", "s_wto",
        "s_xa0", "s_xa1", "s_wm0",
        "s_hT0", "s_hT1", "s_qT0", "s_qT1", "s_kT0", "s_kT1",
        "s_xr0", "s_xr1", "s_out", "vp",
        "t_gemv", "t_mod", "t_bf", "t_qkv", "t_S", "t_Y", "t_PU",
        "t_hs", "t_ab2", "t_r2", "t_A", "t_s", "t_g", "t_o", "s_cb", "t_sbw",
        "sc_silu", "sc_tab", "sc_sq", "sc_mu2", "sc_rho",
        "sc_ve", "sc_ye", "sc_y2", "sc_hp", "sc_g", "sc_hs",
    ]
    S = {n: nc.alloc_semaphore(n) for n in sem_names}
    NC_ALL = 16 * N_CONST_DMAS
    vpos = {}

    with nc.Block() as block:
        @block.vector
        def _(v):
            vn = [0]

            def vop(name, *a, mark=None, **kw):
                if vn[0] > 0:
                    v.wait_ge(S["vp"], vn[0])
                ins = getattr(v, name)(*a, **kw)
                vn[0] += 1
                ins.then_inc(S["vp"], 1)
                if mark is not None:
                    vpos[mark] = vn[0]
                return ins

            v.wait_ge(S["s_cst"], NC_ALL)
            vop("memset", ysum_a[:], 0.0)
            vop("memset", ysq_a[:], 0.0)
            v.wait_ge(S["sc_silu"], 1)
            vop("tensor_copy", silb_sb[:], silf_sb[:], mark="silu")

            def mod_step_v(s):
                j = SLICE_ORDER[s]
                v.wait_ge(S["t_gemv"], s + 1)
                if s >= 1:
                    v.wait_ge(S["t_mod"], s)
                vop("tensor_copy", mrow_sb[0:1, :], ps4[0:1, :], mark=("mrow", s))
                v.wait_ge(S["t_mod"], s + 1)
                vop("tensor_copy", modt[:, 512 * j:512 * (j + 1)], ps4[:, 0:512],
                    mark=("modp", s))

            def gate_adjust(off):
                gsl = modt[:, off:off + 1024]
                vop("tensor_single_scalar", gascr[:], gsl, 0.0, ALU.is_ge)
                vop("tensor_scalar", gascr[:], gascr[:], 2.0, -1.0, ALU.mult, ALU.add)
                vop("tensor_mul", gsl, gsl, gascr[:])
                vop("tensor_scalar_max", gsl, gsl, 0.25)
                vop("tensor_mul", gsl, gsl, gascr[:])

            for s_i in range(4):
                mod_step_v(s_i)
            vop("tensor_scalar_add", modt[:, 1024:2048], modt[:, 1024:2048], 1.0)

            def v_pre(c):
                p = c % 2
                v.wait_ge(S[f"s_xa{p}"], 16 * (c // 2 + 1))
                v.wait_ge(S["sc_sq"], c + 1)
                vop("tensor_scalar", sqs[:], sqs[:], 1.0, None, ALU.mult, ALU.add,
                    accum_out=xsq[:, c:c + 1])
                vop("tensor_scalar", sqs[:], xa[p][:], 1.0, None, ALU.mult, ALU.add,
                    accum_out=xsum[:, c:c + 1])
                vop("tensor_scalar_mul", mu[:, c:c + 1], xsum[:, c:c + 1], 1.0 / D,
                    mark=("mu", c))
                vop("tensor_mul", mu2[:, c:c + 1], mu[:, c:c + 1], mu[:, c:c + 1])
                vop("scalar_tensor_tensor", var[:, c:c + 1], xsq[:, c:c + 1], 1.0 / D,
                    mu2[:, c:c + 1], ALU.mult, ALU.subtract, mark=("var", c))
                v.wait_ge(S["sc_rho"], c + 1)
                vop("reciprocal", rho[:, c:c + 1], rho[:, c:c + 1])
                if c >= 2:
                    v.wait_ge(S[f"s_hT{p}"], 16 * (c // 2))
                vop("tensor_scalar", htl[p][:], xa[p][:], mu[:, c:c + 1], rho[:, c:c + 1],
                    ALU.subtract, ALU.mult, mark=("hA", c))
                vop("tensor_mul", htl[p][:], htl[p][:], modt[:, 1024:2048])
                vop("tensor_add", htl[p][:], htl[p][:], modt[:, 0:1024], mark=("h", c))

            v_pre(0)
            v_pre(1)

            for c in range(KB_NCH):
                p = c % 2
                if 0 <= c <= 7:
                    mod_step_v(c + 4)
                v.wait_ge(S["sc_tab"], c + 1)
                if c >= 2:
                    v.wait_ge(S[f"s_qT{p}"], 16 * (c // 2))
                    v.wait_ge(S[f"s_kT{p}"], 16 * (c // 2))
                    v.wait_ge(S["t_PU"], c - 1)
                v.wait_ge(S["t_qkv"], 6 * c + 1)
                vop("tensor_mul", qtm[p][:, 0:512], ps01[:, 0:512], tabq[p][:, 0:512],
                    mark=("qe", c, 0))
                v.wait_ge(S["t_qkv"], 6 * c + 2)
                vop("tensor_mul", qtm[p][:, 512:1024], ps01[:, 512:1024],
                    tabq[p][:, 512:1024], mark=("qe", c, 1))
                v.wait_ge(S["t_qkv"], 6 * c + 3)
                vop("tensor_mul", ktm[p][:, 0:512], ps01[:, 0:512], tabk[p][:, 0:512],
                    mark=("qe", c, 2))
                if c >= 1:
                    v.wait_ge(S["sc_ye"], c)
                    vop("reduce_sum", ysumc[:],
                        ybf[:, 1024 * (c - 1):1024 * c].rearrange("p (j a) -> p j a", j=8),
                        mybir.AxisListType.X)
                    vop("tensor_add", ysum_a[:], ysum_a[:], ysumc[:])
                v.wait_ge(S["t_qkv"], 6 * c + 4)
                vop("tensor_mul", ktm[p][:, 512:1024], ps01[:, 512:1024],
                    tabk[p][:, 512:1024], mark=("qke", c))
                if c >= 1:
                    v.wait_ge(S["sc_y2"], c)
                    vop("reduce_sum", ysqc[:], y2s[:].rearrange("p (j a) -> p j a", j=8),
                        mybir.AxisListType.X)
                    vop("tensor_add", ysq_a[:], ysq_a[:], ysqc[:], mark=("yst", c - 1))
                    v.wait_ge(S["t_PU"], c)
                    vop("tensor_single_scalar", pbf[:], ps7[:], 0.125,
                        ALU.mult, mark=("Pe", c - 1))
                if c >= 1:
                    v.wait_ge(S["t_Y"], c)
                v.wait_ge(S["t_S"], 4 * c + 1)
                vop("tensor_mul", stl[p][:, 0:512], psS2[:], mask_sb[:, 0:512],
                    mark=("Se", c, 0))
                v.wait_ge(S["t_S"], 4 * c + 2)
                vop("tensor_mul", stl[p][:, 1024:1536], psS3[:], mask_sb[:, 1024:1536],
                    mark=("Se", c, 1))
                v.wait_ge(S["t_S"], 4 * c + 3)
                vop("tensor_mul", stl[p][:, 512:1024], psS2[:], mask_sb[:, 512:1024],
                    mark=("Se", c, 2))
                v.wait_ge(S["t_S"], 4 * c + 4)
                vop("tensor_mul", stl[p][:, 1536:2048], psS3[:], mask_sb[:, 1536:2048],
                    mark=("Se", c))
                if c == 2:
                    gate_adjust(2048)
                if c == 3:
                    v.wait_ge(S["s_wto"], 16 * 8)
                    for dj in range(8):
                        vop("tensor_mul", wt["o"][:, 1024 * dj:1024 * (dj + 1)],
                            wt["o"][:, 1024 * dj:1024 * (dj + 1)],
                            modt[:, 2048:3072], mark=("wog" if dj == 7 else None))
                if c == 4:
                    gate_adjust(5120)
                if c == 5:
                    vop("tensor_scalar_mul", b1pp_sb[:], wp1_sb[:], bind_sb[:, 0:1])
                    vop("tensor_add", b1pp_sb[:], b1pp_sb[:], b1p_sb[:])
                    for jj2 in range(8):
                        vop("tensor_scalar_mul", dg_sb[:, 64 * jj2:64 * (jj2 + 1)],
                            id2_sb[:], w2p_sb[:, jj2:jj2 + 1])
                    vop("tensor_copy", b2b_sb[:], b2f_sb[:])
                    vop("tensor_mul", wout_sb[0:64, :], wout_sb[0:64, :],
                        modt[0:64, 5120:6144])
                    vop("tensor_mul", bout_sb[0:1, :], bout_sb[0:1, :],
                        modt[0:1, 5120:6144], mark="ffn1")
                if c == 6:
                    v.wait_ge(S["t_bf"], 1)
                    vop("tensor_add", bf_sb[0:1, 0:512], ps4[0:1, 0:512],
                        bout_sb[0:1, 0:512], mark="bfh0")
                if c == 7:
                    v.wait_ge(S["t_bf"], 2)
                    vop("tensor_add", bf_sb[0:1, 512:1024], ps4[0:1, 0:512],
                        bout_sb[0:1, 512:1024], mark="ffn2")
                if c == 8:
                    vop("tensor_scalar_add", modt[:, 4096:5120], modt[:, 4096:5120], 1.0,
                        mark="cb1")
                if c + 2 < KB_NCH:
                    v_pre(c + 2)

            v.wait_ge(S["sc_ye"], NCH)
            vop("reduce_sum", ysumc[:],
                ybf[:, 1024 * (NCH - 1):1024 * NCH].rearrange("p (j a) -> p j a", j=8),
                mybir.AxisListType.X)
            vop("tensor_add", ysum_a[:], ysum_a[:], ysumc[:])
            v.wait_ge(S["sc_y2"], NCH)
            vop("reduce_sum", ysqc[:], y2s[:].rearrange("p (j a) -> p j a", j=8),
                mybir.AxisListType.X)
            vop("tensor_add", ysq_a[:], ysq_a[:], ysqc[:], mark=("yst", NCH - 1))
            v.wait_ge(S["t_PU"], NCH)
            vop("tensor_single_scalar", pbf[:], ps7[:], 0.125,
                ALU.mult, mark=("Pe", NCH - 1))

            if not KB_PH3:
                return
            v.wait_ge(S["t_hs"], 1)
            vop("tensor_copy", hstat[0:2, :], ps4[0:2, 0:16])
            vop("tensor_scalar_mul", hsc[0:2, 0:8], hstat[0:2, 0:8], 1.0 / (DH * T),
                mark="mu2b")
            vop("tensor_mul", hsc[0:2, 24:32], hsc[0:2, 0:8], hsc[0:2, 0:8])
            vop("scalar_tensor_tensor", hsc[0:2, 16:24], hstat[0:2, 8:16], 1.0 / (DH * T),
                hsc[0:2, 24:32], ALU.mult, ALU.subtract, mark="var2b")
            v.wait_ge(S["sc_hs"], 1)
            vop("reciprocal", hsc[0:2, 24:32], hsc[0:2, 24:32])
            vop("tensor_mul", hsc[0:2, 8:16], hsc[0:2, 0:8], hsc[0:2, 24:32])
            vop("tensor_scalar_mul", hsc[0:2, 8:16], hsc[0:2, 8:16], -1.0, mark="ab")
            v.wait_ge(S["t_ab2"], 1)
            vop("tensor_copy", arep[:], ps4[:, 0:8])
            vop("tensor_copy", brep[:], ps4[:, 8:16])
            vop("tensor_mul", arep[:], arep[:], gnwj_sb[:])
            vop("tensor_mul", brep[:], brep[:], gnwj_sb[:])
            vop("tensor_add", brep[:], brep[:], gnbj_sb[:])
            vop("tensor_copy", bbf[:], brep[:], mark="AB")
            v.wait_ge(S["t_r2"], 1)
            for dj in range(8):
                vop("tensor_scalar_mul", wt["o"][:, 1024 * dj:1024 * (dj + 1)],
                    wt["o"][:, 1024 * dj:1024 * (dj + 1)], arep[:, dj:dj + 1])
            vop("tensor_copy", row2_sb[0:1, 0:512], ps4[0:1, 0:512])
            vop("tensor_copy", row2_sb[0:1, 512:1024], ps7[0:1, 0:512], mark="woa")
            v.wait_ge(S["t_sbw"], 1)
            vop("tensor_copy", sbw_sb[:], psS3[:, 0:1])
            vop("tensor_scalar", arep[:], wp1_sb[:], sbw_sb[:, 0:1], None,
                ALU.mult, ALU.bypass)
            vop("tensor_add", b1pp_sb[:], b1pp_sb[:], arep[:], mark="b1adj")
            v.wait_ge(S["s_cb"], 48)
            vop("tensor_copy", cbT_sb[:], cbTb_sb[:])
            for dc in range(NDC):
                vop("tensor_scalar_mul", windup_sb[:, 128 * dc:128 * (dc + 1)],
                    windup_sb[:, 128 * dc:128 * (dc + 1)], cbT_sb[:, dc:dc + 1],
                    mark=("wfold" if dc == NDC - 1 else None))

            def v3_pre(tau):
                p = tau % 2
                v.wait_ge(S[f"s_xr{p}"], 16 * (tau // 2 + 1))
                v.wait_ge(S["t_A"], 2 * tau + 1)
                vop("scalar_tensor_tensor", x2[p][:, 0:512], ps01[:, 0:512], 1.0,
                    xa[p][:, 0:512], ALU.mult, ALU.add, mark=("x2h", tau, 0))
                v.wait_ge(S["t_A"], 2 * tau + 2)
                vop("scalar_tensor_tensor", x2[p][:, 512:1024], ps01[:, 512:1024], 1.0,
                    xa[p][:, 512:1024], ALU.mult, ALU.add, mark=("x2h", tau, 1))
                vpos[("x2", tau)] = vpos[("x2h", tau, 1)]
                v.wait_ge(S["sc_sq"], 17 + tau)
                vop("tensor_scalar", sqs[:], sqs[:], 1.0, None, ALU.mult, ALU.add,
                    accum_out=xsq[:, tau:tau + 1])
                vop("tensor_scalar", sqs[:], x2[p][:], 1.0, None, ALU.mult, ALU.add,
                    accum_out=xsum[:, tau:tau + 1])
                vop("tensor_scalar_mul", mu[:, tau:tau + 1], xsum[:, tau:tau + 1],
                    1.0 / D, mark=("mu", NCH + tau))
                vop("tensor_mul", mu2[:, tau:tau + 1], mu[:, tau:tau + 1],
                    mu[:, tau:tau + 1])
                vop("scalar_tensor_tensor", var[:, tau:tau + 1], xsq[:, tau:tau + 1],
                    1.0 / D, mu2[:, tau:tau + 1], ALU.mult, ALU.subtract,
                    mark=("var", NCH + tau))
                v.wait_ge(S["sc_rho"], 17 + tau)
                vop("reciprocal", rho[:, tau:tau + 1], rho[:, tau:tau + 1])
                v.wait_ge(S[f"s_hT{p}"], 16 * (8 + tau // 2))
                vop("tensor_scalar", htl[p][:], x2[p][:], mu[:, tau:tau + 1],
                    rho[:, tau:tau + 1], ALU.subtract, ALU.mult, mark=("hA", NCH + tau))
                vpos[("h", NCH + tau)] = vpos[("hA", NCH + tau)]

            v3_pre(0)
            v3_pre(1)
            for tau in range(NCH):
                p = tau % 2
                v.wait_ge(S["t_o"], tau + 1)
                vop("scalar_tensor_tensor", xa[p][:], psY[:], 1.0, x2[p][:],
                    ALU.mult, ALU.add, mark=("out", tau))
                if tau + 2 < NCH:
                    v3_pre(tau + 2)

        @block.sync
        def _(s):
            def chain():
                pass

            for i, nm in enumerate(("q", "k", "v", "o")):
                s.wait_ge(S[f"s_wc{i}"], 16)
                for oc in range(8):
                    dst = wt[nm][:].rearrange("p (dj oc b) -> p dj oc b", dj=8, oc=8)[:, :, oc, :]
                    s.dma_start_transpose(dst, wsc[i][128 * oc:128 * (oc + 1), :]).then_inc(
                        S[f"s_wt{nm}"], 16)
            for c in range(KB_NCH):
                p = c % 2
                if c == 10:
                    s.wait_ge(S["vp"], vpos["cb1"])
                    s.dma_start(cbs_d[0:1, :], modt[0:1, 4096:5120]).then_inc(S["s_cb"], 16)
                    s.wait_ge(S["s_cb"], 16)
                    s.dma_start(cbs_d[1:2, :], modt[0:1, 3072:4096]).then_inc(S["s_cb"], 16)
                    s.wait_ge(S["s_cb"], 32)
                    s.dma_start_transpose(
                        cbTb_sb[:].rearrange("p (j b) -> p j b", j=8),
                        cbs_d[0:1, :],
                    ).then_inc(S["s_cb"], 16)
                    s.wait_ge(S["s_cb"], 48)
                    s.dma_start_transpose(
                        sbT_sb[:].rearrange("p (j b) -> p j b", j=8),
                        cbs_d[1:2, :],
                    ).then_inc(S["s_cb"], 16)
                s.wait_ge(S["vp"], vpos[("h", c)])
                if c >= 2:
                    s.wait_ge(S["t_qkv"], 6 * (c - 1))
                s.dma_start_transpose(
                    hT[p][:].rearrange("p (dj b) -> p dj b", dj=8), htl[p][:]
                ).then_inc(S[f"s_hT{p}"], 16)
                if KB_STAGE < 2:
                    continue
                s.wait_ge(S["vp"], vpos[("qe", c, 1)])
                if c >= 2:
                    s.wait_ge(S["t_Y"], c - 1)
                s.dma_start_transpose(
                    qfm[p][:].rearrange("p (dj b) -> p dj b", dj=8), qtm[p][:]
                ).then_inc(S[f"s_qT{p}"], 16)
                s.wait_ge(S["vp"], vpos[("qke", c)])
                s.dma_start_transpose(
                    kfm[p][:].rearrange("p (dj b) -> p dj b", dj=8), ktm[p][:]
                ).then_inc(S[f"s_kT{p}"], 16)
            if not KB_PH3:
                return
            for tau in range(NCH):
                p = tau % 2
                s.wait_ge(S["vp"], vpos[("h", NCH + tau)])
                if tau >= 2:
                    s.wait_ge(S["t_s"], tau - 1)
                else:
                    s.wait_ge(S["t_qkv"], 6 * (15 + tau))
                s.dma_start_transpose(
                    hT[p][:].rearrange("p (dj b) -> p dj b", dj=8), htl[p][:]
                ).then_inc(S[f"s_hT{p}"], 16)
                s.wait_ge(S["vp"], vpos[("out", tau)])
                if tau >= 1:
                    s.wait_ge(S["s_out"], 16 * tau)
                s.dma_start(out_d[128 * tau:128 * (tau + 1), :], xa[p][:]).then_inc(
                    S["s_out"], 16)
            s.wait_ge(S["s_out"], 16 * NCH)

        @block.tensor
        def _(t):
            t.wait_ge(S["s_cst"], NC_ALL)
            t.wait_ge(S["vp"], vpos["silu"])

            def mod_step_t(s):
                t.wait_ge(S["s_wm0"], 16 * (s + 1))
                if s >= 1:
                    t.wait_ge(S["vp"], vpos[("modp", s - 1)])
                if s == 11:
                    t.wait_ge(S["vp"], vpos["bfh0"])
                for dc in range(NDC):
                    mm2 = t.matmul(ps4[0:1, :], silb_sb[:, dc:dc + 1],
                                   wmsl[:, 512 * dc:512 * (dc + 1)],
                                   start=(dc == 0), stop=(dc == NDC - 1))
                mm2.then_inc(S["t_gemv"], 1)
                t.wait_ge(S["vp"], vpos[("mrow", s)])
                t.matmul(ps4[:, 0:512], ones_sb[0:1, :], mrow_sb[0:1, :],
                         start=True, stop=True).then_inc(S["t_mod"], 1)

            for s_i in range(4):
                mod_step_t(s_i)

            for c in range(KB_NCH):
                p = c % 2
                if 0 <= c <= 7:
                    mod_step_t(c + 4)
                if c == 6:
                    t.wait_ge(S["vp"], vpos["ffn1"])
                    t.wait_ge(S["vp"], vpos[("modp", 10)])
                    t.matmul(ps4[0:1, 0:512], b2b_sb[:, 0:1],
                             wout_sb[0:64, 0:512], start=True,
                             stop=True).then_inc(S["t_bf"], 1)
                if c == 7:
                    t.wait_ge(S["vp"], vpos["bfh0"])
                    t.wait_ge(S["vp"], vpos[("modp", 11)])
                    t.matmul(ps4[0:1, 0:512], b2b_sb[:, 0:1],
                             wout_sb[0:64, 512:1024], start=True,
                             stop=True).then_inc(S["t_bf"], 1)
                if c == 0:
                    for nm in ("q", "k", "v"):
                        t.wait_ge(S[f"s_wt{nm}"], 16 * 8)
                t.wait_ge(S[f"s_hT{p}"], 16 * (c // 2 + 1))
                for g, (nm, hf) in enumerate(
                        (("q", 0), ("q", 1), ("k", 0), ("k", 1), ("v", 0), ("v", 1))):
                    if g == 0:
                        if c >= 1:
                            t.wait_ge(S["sc_ve"], 2 * c - 1)
                    elif g == 1:
                        if c >= 1:
                            t.wait_ge(S["sc_ve"], 2 * c)
                    elif g == 5:
                        t.wait_ge(S["vp"], vpos[("qke", c)])
                    else:
                        t.wait_ge(S["vp"], vpos[("qe", c, g - 2)])
                    for dc in range(NDC):
                        mm = t.matmul(
                            ps01[:, 512 * hf:512 * (hf + 1)],
                            hT[p][:, 128 * dc:128 * (dc + 1)],
                            wt[nm][:, 1024 * dc + 512 * hf:1024 * dc + 512 * (hf + 1)],
                            start=(dc == 0), stop=(dc == NDC - 1))
                    mm.then_inc(S["t_qkv"], 1)

                if KB_STAGE < 3:
                    continue
                t.wait_ge(S[f"s_qT{p}"], 16 * (c // 2 + 1))
                t.wait_ge(S[f"s_kT{p}"], 16 * (c // 2 + 1))
                for pss in range(4):
                    gg = pss % 2
                    half = pss // 2
                    psS = psS2 if gg == 0 else psS3
                    if half == 1:
                        t.wait_ge(S["vp"], vpos[("Se", c, pss - 2)])
                    elif c >= 1:
                        t.wait_ge(S["vp"],
                                  vpos[("Se", c - 1, 2)] if gg == 0
                                  else vpos[("Se", c - 1)])
                    for r in range(4):
                        jj = 4 * half + r
                        tp = (64 * gg, 0) if gg else None
                        mm = t.matmul(
                            psS[:, 128 * r:128 * (r + 1)],
                            kfm[p][64 * gg:64 * (gg + 1), 128 * jj:128 * (jj + 1)],
                            qfm[p][64 * gg:64 * (gg + 1), 128 * jj:128 * (jj + 1)],
                            start=(r == 0), stop=(r == 3), tile_position=tp)
                    mm.then_inc(S["t_S"], 1)

                if KB_STAGE < 5:
                    continue
                started = {}
                if c > 0:
                    t.wait_ge(S["vp"], vpos[("Pe", c - 1)])
                    t.wait_ge(S["sc_y2"], c)
                t.wait_ge(S["vp"], vpos[("Se", c)])
                t.wait_ge(S["sc_ve"], 2 * c + 2)
                if c > 0:
                    for h in [2 * j for j in range(8)] + [2 * j + 1 for j in range(8)]:
                        jj, gg = h // 2, h % 2
                        bank = 4 + jj // 4
                        st = not started.get((bank, gg), False)
                        started[(bank, gg)] = True
                        tp = (64 * gg, 64 * gg) if gg else None
                        t.matmul(psY[64 * gg:64 * (gg + 1), 128 * jj:128 * (jj + 1)],
                                 pbf[64 * gg:64 * (gg + 1), 64 * jj:64 * (jj + 1)],
                                 qfm[p][64 * gg:64 * (gg + 1), 128 * jj:128 * (jj + 1)],
                                 start=st, stop=False, tile_position=tp,
                                 skip_group_check=True)
                for h in range(H):
                    jj, gg = h // 2, h % 2
                    bank = 4 + jj // 4
                    st = not started.get((bank, gg), False)
                    started[(bank, gg)] = True
                    tp = (0, 64 * gg) if gg else None
                    scol = 1024 * gg + 128 * jj
                    mm = t.matmul(psY[64 * gg:64 * (gg + 1), 128 * jj:128 * (jj + 1)],
                                  vtm[p][:, 64 * h:64 * (h + 1)],
                                  stl[p][:, scol:scol + 128],
                                  start=st, stop=(jj % 4 == 3), tile_position=tp,
                                  skip_group_check=True)
                mm.then_inc(S["t_Y"], 1)
                if KB_STAGE < 6:
                    continue
                for h in range(H):
                    jj, gg = h // 2, h % 2
                    tp = (0, 64 * gg) if gg else None
                    mm = t.matmul(ps7[64 * gg:64 * (gg + 1), 64 * jj:64 * (jj + 1)],
                                  ktm[p][:, 64 * h:64 * (h + 1)],
                                  vtm[p][:, 64 * h:64 * (h + 1)],
                                  start=(c == 0 and jj == 0),
                                  stop=(c == NCH - 1 and jj == 7), tile_position=tp,
                                  skip_group_check=True)
                mm.then_inc(S["t_PU"], 1)

            if not KB_PH3:
                return
            t.wait_ge(S["vp"], vpos[("yst", NCH - 1)])
            t.matmul(ps4[0:2, 0:8], hind_sb[:], ysum_a[:], start=True, stop=False)
            t.matmul(ps4[0:2, 8:16], hind_sb[:], ysq_a[:], start=False,
                     stop=True).then_inc(S["t_hs"], 1)
            t.wait_ge(S["vp"], vpos["ab"])
            t.matmul(ps4[:, 0:8], hindT_sb[:], hsc[0:2, 24:32], start=True, stop=False)
            t.matmul(ps4[:, 8:16], hindT_sb[:], hsc[0:2, 8:16], start=False,
                     stop=True).then_inc(S["t_ab2"], 1)
            t.wait_ge(S["vp"], vpos["AB"])
            t.wait_ge(S["vp"], vpos["wog"])
            t.wait_ge(S["vp"], vpos[("Pe", NCH - 1)])
            for hf in range(2):
                psr = ps4 if hf == 0 else ps7
                for dj in range(8):
                    mm = t.matmul(psr[0:1, 0:512], bbf[:, dj:dj + 1],
                                  wt["o"][:, 1024 * dj + 512 * hf:1024 * dj + 512 * (hf + 1)],
                                  start=(dj == 0), stop=(dj == 7))
            mm.then_inc(S["t_r2"], 1)

            t.wait_ge(S["vp"], vpos["woa"])
            t.wait_ge(S["sc_ye"], NCH)
            t.wait_ge(S["vp"], vpos[("Se", NCH - 1)])
            t.wait_ge(S["s_cb"], 64)
            for dc in range(NDC):
                mm = t.matmul(psS3[:, 0:1], windup_sb[:, 128 * dc:128 * (dc + 1)],
                              sbT_sb[:, dc:dc + 1], start=(dc == 0), stop=(dc == NDC - 1))
            mm.then_inc(S["t_sbw"], 1)
            def p3_A(tau):
                for hf in range(2):
                    if tau >= 1:
                        t.wait_ge(S["vp"], vpos[("x2h", tau - 1, hf)])
                    for dj in range(8):
                        t.matmul(ps01[:, 512 * hf:512 * (hf + 1)],
                                 ybf[:, 1024 * tau + 128 * dj:1024 * tau + 128 * (dj + 1)],
                                 wt["o"][:, 1024 * dj + 512 * hf:1024 * dj + 512 * (hf + 1)],
                                 start=(dj == 0), stop=False)
                    mm = t.matmul(ps01[:, 512 * hf:512 * (hf + 1)], ones_sb[0:1, :],
                                  row2_sb[0:1, 512 * hf:512 * (hf + 1)], start=False, stop=True)
                    mm.then_inc(S["t_A"], 1)

            def p3_s(tau):
                p = tau % 2
                t.wait_ge(S[f"s_hT{p}"], 16 * (8 + tau // 2 + 1))
                if tau == 0:
                    t.wait_ge(S["vp"], vpos["wfold"])
                if tau >= 1:
                    t.wait_ge(S["sc_hp"], tau)
                for dc in range(NDC):
                    mm = t.matmul(psS2[:, 0:128], windup_sb[:, 128 * dc:128 * (dc + 1)],
                                  hT[p][:, 128 * dc:128 * (dc + 1)],
                                  start=(dc == 0), stop=(dc == NDC - 1))
                mm.then_inc(S["t_s"], 1)

            def p3_go(tau):
                p = tau % 2
                t.wait_ge(S["sc_hp"], tau + 1)
                if tau >= 1:
                    t.wait_ge(S["sc_g"], tau)
                for j in range(8):
                    mm = t.matmul(psS3[0:64, 0:128], dg_sb[:, 64 * j:64 * (j + 1)],
                                  qtm[p][:, 128 * j:128 * (j + 1)],
                                  start=(j == 0), stop=(j == 7))
                mm.then_inc(S["t_g"], 1)
                t.wait_ge(S["sc_g"], tau + 1)
                if tau >= 1:
                    t.wait_ge(S["vp"], vpos[("out", tau - 1)])
                for hf in range(2):
                    t.matmul(psY[:, 512 * hf:512 * (hf + 1)], gb1[p][:],
                             wout_sb[0:64, 512 * hf:512 * (hf + 1)], start=True, stop=False)
                    mm = t.matmul(psY[:, 512 * hf:512 * (hf + 1)], ones_sb[0:1, :],
                                  bf_sb[0:1, 512 * hf:512 * (hf + 1)], start=False, stop=True)
                mm.then_inc(S["t_o"], 1)

            p3_A(0)
            p3_A(1)
            for tau in range(NCH):
                p3_s(tau)
                if tau + 2 < NCH:
                    p3_A(tau + 2)
                p3_go(tau)

        @block.scalar
        def _(sc):
            sc.wait_ge(S["s_cst"], NC_ALL)
            sc.activation(silf_sb[:], cfm_sb[:], AF.Silu).then_inc(S["sc_silu"], 1)

            def sc_tab(c):
                p = c % 2
                if c >= 2:
                    sc.wait_ge(S["vp"], vpos[("qke", c - 2)])
                sc.activation(tabq[p][:], lng_sb[:], AF.Exp,
                              bias=0.0, scale=tvec_sb[:, c:c + 1])
                sc.activation(tabk[p][:], lng_sb[:], AF.Exp,
                              bias=0.0, scale=ntv_sb[:, c:c + 1]).then_inc(S["sc_tab"], 1)

            def sc_sq(c):
                p = c % 2
                sc.wait_ge(S[f"s_xa{p}"], 16 * (c // 2 + 1))
                if c >= 1:
                    sc.wait_ge(S["vp"], vpos[("mu", c - 1)])
                sc.square(sqs[:], xa[p][:]).then_inc(S["sc_sq"], 1)

            def sc_rho(c):
                sc.wait_ge(S["vp"], vpos[("var", c)])
                sc.activation(rho[:, c:c + 1], var[:, c:c + 1], AF.Sqrt,
                              bias=actb_sb[:, 1:2], scale=1.0).then_inc(S["sc_rho"], 1)

            sc_tab(0)
            sc_sq(0)
            sc_rho(0)
            sc_tab(1)
            sc_sq(1)
            sc_rho(1)
            for c in range(KB_NCH):
                p = c % 2
                if c + 2 < KB_NCH:
                    sc_sq(c + 2)
                    sc_tab(c + 2)
                sc.wait_ge(S["t_qkv"], 6 * c + 5)
                if c >= 2:
                    sc.wait_ge(S["t_PU"], c - 1)
                sc.copy(vtm[p][:, 0:512], ps01[:, 0:512]).then_inc(S["sc_ve"], 1)
                sc.wait_ge(S["t_qkv"], 6 * c + 6)
                sc.copy(vtm[p][:, 512:1024], ps01[:, 512:1024]).then_inc(S["sc_ve"], 1)
                if c + 2 < KB_NCH:
                    sc_rho(c + 2)
                sc.wait_ge(S["t_Y"], c + 1)
                sc.copy(ybf[:, 1024 * c:1024 * (c + 1)], psY[:]).then_inc(S["sc_ye"], 1)
                if c >= 1:
                    sc.wait_ge(S["vp"], vpos[("yst", c - 1)])
                sc.square(y2s[:], psY[:]).then_inc(S["sc_y2"], 1)
            if not KB_PH3:
                return
            sc.wait_ge(S["vp"], vpos["var2b"])
            sc.activation(hsc[0:2, 24:32], hsc[0:2, 16:24], AF.Sqrt,
                          bias=actb_sb[0:2, 1:2], scale=1.0).then_inc(S["sc_hs"], 1)
            for tau in range(NCH):
                p = tau % 2
                sc.wait_ge(S["vp"], vpos[("x2", tau)])
                if tau == 0:
                    sc.wait_ge(S["vp"], vpos["mu2b"])
                else:
                    sc.wait_ge(S["vp"], vpos[("mu", NCH + tau - 1)])
                sc.square(sqs[:], x2[p][:]).then_inc(S["sc_sq"], 1)
                sc.wait_ge(S["vp"], vpos[("var", NCH + tau)])
                sc.activation(rho[:, tau:tau + 1], var[:, tau:tau + 1], AF.Sqrt,
                              bias=actb_sb[:, 1:2], scale=1.0).then_inc(S["sc_rho"], 1)
                sc.wait_ge(S["t_s"], tau + 1)
                if tau == 0:
                    sc.wait_ge(S["vp"], vpos["b1adj"])
                if tau >= 2:
                    sc.wait_ge(S["t_g"], tau - 1)
                else:
                    sc.wait_ge(S[f"s_qT{p}"], 16 * 8)
                for j in range(8):
                    act = sc.activation(qtm[p][:, 128 * j:128 * (j + 1)], psS2[:, 0:128],
                                        AF.Relu, bias=b1pp_sb[:, j:j + 1],
                                        scale=wp1_sb[:, j:j + 1])
                act.then_inc(S["sc_hp"], 1)
                sc.wait_ge(S["t_g"], tau + 1)
                if tau >= 2:
                    sc.wait_ge(S["t_o"], tau - 1)
                sc.copy(gb1[p][:], psS3[0:64, 0:128]).then_inc(S["sc_g"], 1)

        @block.gpsimd
        def _(g):
            g.dma_start(mask_sb[:], mask_d[:, :]).then_inc(S["s_cst"], 16)
            g.dma_start(lng_sb[:], lng_d[:, :].to_broadcast((128, D))).then_inc(S["s_cst"], 16)
            g.dma_start(ntv_sb[:], ntv_d[:, :]).then_inc(S["s_cst"], 16)
            g.dma_start(tvec_sb[:], tvec_d[:, :]).then_inc(S["s_cst"], 16)
            g.dma_start(id2_sb[:], id2_d[:, :]).then_inc(S["s_cst"], 16)
            g.dma_start(ones_sb[:], ones1_d[:, :]).then_inc(S["s_cst"], 16)
            g.dma_start(hind_sb[:], hind_d[:, :]).then_inc(S["s_cst"], 16)
            g.dma_start(hindT_sb[:], hindT_d[:, :]).then_inc(S["s_cst"], 16)
            g.dma_start(actb_sb[:], actb_d[:, :]).then_inc(S["s_cst"], 16)
            g.dma_start(gnwj_sb[:], gnw_d[:, :]).then_inc(S["s_cst"], 16)
            g.dma_start(gnbj_sb[:], gnb_d[:, :]).then_inc(S["s_cst"], 16)
            g.dma_start(wp1_sb[:], w1p_d[:, :]).then_inc(S["s_cst"], 16)
            g.dma_start(b1p_sb[:], b1p_d[:, :]).then_inc(S["s_cst"], 16)
            g.dma_start(w2p_sb[:], w2p_d[:, :]).then_inc(S["s_cst"], 16)
            g.dma_start(bind_sb[0:64, :], bin_d[:, :]).then_inc(S["s_cst"], 16)
            g.dma_start(bind_sb[64:128, :], bin_d[:, :]).then_inc(S["s_cst"], 16)
            g.dma_start(b2f_sb[:], b2_d[:, :]).then_inc(S["s_cst"], 16)
            g.dma_start(bout_sb[:], bout_d[:, :]).then_inc(S["s_cst"], 16)
            g.dma_start(cfm_sb[:], crow_d[:, :]).then_inc(S["s_cst"], 16)
            g.dma_start(wout_sb[:], wout_d[:, :]).then_inc(S["s_cst"], 16)
            g.dma_start(
                windup_sb[:].rearrange("p (c m) -> p c m", c=NDC)[:, :, 0:64],
                win_d[:, :].rearrange("(c p) m -> p c m", c=NDC),
            ).then_inc(S["s_cst"], 16)
            g.dma_start(
                windup_sb[:].rearrange("p (c m) -> p c m", c=NDC)[:, :, 64:128],
                win_d[:, :].rearrange("(c p) m -> p c m", c=NDC),
            ).then_inc(S["s_cst"], 16)

            for i, wd in enumerate((wq_d, wk_d, wv_d, wo_d)):
                g.dma_start(wsc[i][:, :], wd[:, :]).then_inc(S[f"s_wc{i}"], 16)

            def wm_load(s):
                j = SLICE_ORDER[s]
                if s >= 1:
                    g.wait_ge(S["t_gemv"], s)
                g.dma_start(
                    wmsl[:].rearrange("p (c o) -> p c o", c=NDC),
                    wmod_d[:, :].rearrange("(c p) o -> p c o", c=NDC)[:, :, 512 * j:512 * (j + 1)],
                ).then_inc(S["s_wm0"], 16)

            def x_load(c):
                if c >= 2:
                    g.wait_ge(S["vp"], vpos[("hA", c - 2)])
                g.dma_start(xa[c % 2][:], x_d[128 * c:128 * (c + 1), :]).then_inc(
                    S[f"s_xa{c % 2}"], 16)

            wm_load(0); wm_load(1)
            x_load(0)
            wm_load(2); wm_load(3)
            x_load(1)
            for c in range(2, NCH):
                if c + 2 < 12:
                    wm_load(c + 2)
                x_load(c)

            if KB_PH3:
                for tau in range(NCH):
                    if tau <= 1:
                        g.wait_ge(S["vp"], vpos[("hA", 14 + tau)])
                    else:
                        g.wait_ge(S["s_out"], 16 * (tau - 1))
                    g.dma_start(xa[tau % 2][:], x_d[128 * tau:128 * (tau + 1), :]).then_inc(
                        S[f"s_xr{tau % 2}"], 16)
                g.wait_ge(S["s_out"], 16 * NCH)

    return nc


_NC_CACHE = None


def kernel(L0_attn_x, L0_attn_W_q, L0_attn_W_k, L0_attn_W_v, L0_attn_W_o,
           L0_attn_gn_weight, L0_attn_gn_bias, L0_ffn_Win, L0_ffn_bin_,
           L0_ffn_W1, L0_ffn_b1, L0_ffn_W2, L0_ffn_b2, L0_ffn_Wout,
           L0_ffn_bout, L0_c, L0_wmod):
    global _NC_CACHE
    if _NC_CACHE is None:
        _NC_CACHE = build_nc()
    nc = _NC_CACHE

    f = np.float32
    x = np.ascontiguousarray(np.asarray(L0_attn_x, dtype=f))
    cmat = np.ascontiguousarray(np.asarray(L0_c, dtype=f))
    W1 = np.asarray(L0_ffn_W1, dtype=f)
    b1 = np.asarray(L0_ffn_b1, dtype=f)
    W2 = np.asarray(L0_ffn_W2, dtype=f)
    pair = lambda M: np.ascontiguousarray(
        M.reshape(64, 8, 2).transpose(2, 0, 1).reshape(128, 8))

    base = {
        "wq": np.ascontiguousarray(np.asarray(L0_attn_W_q, dtype=f)),
        "wk": np.ascontiguousarray(np.asarray(L0_attn_W_k, dtype=f)),
        "wv": np.ascontiguousarray(np.asarray(L0_attn_W_v, dtype=f)),
        "wo": np.ascontiguousarray(np.asarray(L0_attn_W_o, dtype=f)),
        "gnw": np.ascontiguousarray(np.asarray(L0_attn_gn_weight, dtype=f).reshape(8, 128).T),
        "gnb": np.ascontiguousarray(np.asarray(L0_attn_gn_bias, dtype=f).reshape(8, 128).T),
        "win": np.ascontiguousarray(np.asarray(L0_ffn_Win, dtype=f)),
        "binp": np.ascontiguousarray(np.asarray(L0_ffn_bin_, dtype=f).reshape(64, 1)),
        "w1p": pair(W1), "b1p": pair(b1), "w2p": pair(W2),
        "b2": np.ascontiguousarray(np.asarray(L0_ffn_b2, dtype=f).reshape(64, 1)),
        "wout": np.ascontiguousarray(np.asarray(L0_ffn_Wout, dtype=f)),
        "bout": np.ascontiguousarray(np.asarray(L0_ffn_bout, dtype=f).reshape(1, D)),
        "wmod": np.ascontiguousarray(np.asarray(L0_wmod, dtype=f)),
    }
    in_maps = []
    for b in range(8):
        m = dict(base)
        m["x"] = np.ascontiguousarray(x[b])
        m["crow"] = np.ascontiguousarray(cmat[b].reshape(8, 128).T)
        in_maps.append(m)

    res = run_bass_kernel_spmd(nc, in_maps, core_ids=list(range(8)))
    return np.stack([r["out"] for r in res.results], axis=0)


# version 24 (final)
# speedup vs baseline: 1.1170x; 1.1170x over previous
import numpy as np
import ml_dtypes
import concourse.bass as bass
import concourse.mybir as mybir
from concourse.bass_utils import run_bass_kernel_spmd

F32 = mybir.dt.float32
BF16 = mybir.dt.bfloat16
AF = mybir.ActivationFunctionType
ALU = mybir.AluOpType

T, D, H, DH = 2048, 1024, 16, 64
C = 128
NCH = T // C
NDC = D // 128
GAMMAS = np.array([1.0 - 2.0 ** (-5.0 - i) for i in range(H)], dtype=np.float64)
N_CONST_DMAS = 22
KB_NCH = 16
KB_PH3 = True
KB_MOD = 12
KB_STAGE = 99
KB_SH = 16
KB_ST7 = 3
SLICE_ORDER = [0, 1, 2, 3, 4, 5, 10, 11, 6, 7, 8, 9]


def _consts():
    mask = np.zeros((128, 16 * 128), dtype=np.float32)
    blk = np.triu(np.ones((128, 128), dtype=np.float32)) * 0.125
    for h in range(16):
        mask[:, 128 * h:128 * (h + 1)] = blk
    lng = np.repeat(np.log(GAMMAS), DH).reshape(1, D).astype(np.float32)
    tvec = (np.arange(16)[None, :] * 128 + np.arange(128)[:, None]).astype(np.float32)
    ntvec = -tvec
    id2 = (np.arange(64)[None, :] == (np.arange(128)[:, None] % 64)).astype(np.float32)
    ones1 = np.ones((1, 128), dtype=np.float32)
    actb = np.zeros((128, 2), dtype=np.float32)
    actb[:, 0] = -np.log(8.0)
    actb[:, 1] = 1e-5
    halfind = np.zeros((128, 2), dtype=np.float32)
    halfind[:64, 0] = 1.0
    halfind[64:, 1] = 1.0
    return dict(
        mask=mask.astype(ml_dtypes.bfloat16),
        lng=lng, tvec=tvec, ntvec=ntvec,
        id2=id2.astype(ml_dtypes.bfloat16),
        ones1=ones1.astype(ml_dtypes.bfloat16),
        halfind=halfind, halfindT=np.ascontiguousarray(halfind.T), actb=actb,
    )


def build_nc():
    nc = bass.Bass("TRN2", target_bir_lowering=False)
    cst = _consts()

    x_d = nc.dram_tensor("x", [T, D], F32, kind="ExternalInput").ap()
    wq_d = nc.dram_tensor("wq", [D, D], F32, kind="ExternalInput").ap()
    wk_d = nc.dram_tensor("wk", [D, D], F32, kind="ExternalInput").ap()
    wv_d = nc.dram_tensor("wv", [D, D], F32, kind="ExternalInput").ap()
    wo_d = nc.dram_tensor("wo", [D, D], F32, kind="ExternalInput").ap()
    gnw_d = nc.dram_tensor("gnw", [128, 8], F32, kind="ExternalInput").ap()
    gnb_d = nc.dram_tensor("gnb", [128, 8], F32, kind="ExternalInput").ap()
    win_d = nc.dram_tensor("win", [D, 64], F32, kind="ExternalInput").ap()
    bin_d = nc.dram_tensor("binp", [64, 1], F32, kind="ExternalInput").ap()
    w1p_d = nc.dram_tensor("w1p", [128, 8], F32, kind="ExternalInput").ap()
    b1p_d = nc.dram_tensor("b1p", [128, 8], F32, kind="ExternalInput").ap()
    w2p_d = nc.dram_tensor("w2p", [128, 8], F32, kind="ExternalInput").ap()
    b2_d = nc.dram_tensor("b2", [64, 1], F32, kind="ExternalInput").ap()
    wout_d = nc.dram_tensor("wout", [64, D], F32, kind="ExternalInput").ap()
    bout_d = nc.dram_tensor("bout", [1, D], F32, kind="ExternalInput").ap()
    crow_d = nc.dram_tensor("crow", [128, 8], F32, kind="ExternalInput").ap()
    wmod_d = nc.dram_tensor("wmod", [D, 6 * D], F32, kind="ExternalInput").ap()
    out_d = nc.dram_tensor("out", [T, D], F32, kind="ExternalOutput").ap()

    wsc = [nc.dram_tensor(f"wsc{i}", [D, D], BF16, kind="Internal").ap()
           for i in range(4)]

    mask_d = nc.inline_tensor(cst["mask"], name="c_mask").ap()
    lng_d = nc.inline_tensor(cst["lng"], name="c_lng").ap()
    ntv_d = nc.inline_tensor(cst["ntvec"], name="c_ntvec").ap()
    tvec_d = nc.inline_tensor(cst["tvec"], name="c_tvec").ap()
    id2_d = nc.inline_tensor(cst["id2"], name="c_id2").ap()
    ones1_d = nc.inline_tensor(cst["ones1"], name="c_ones1").ap()
    hind_d = nc.inline_tensor(cst["halfind"], name="c_hind").ap()
    hindT_d = nc.inline_tensor(cst["halfindT"], name="c_hindT").ap()
    actb_d = nc.inline_tensor(cst["actb"], name="c_actb").ap()
    cbs_d = nc.dram_tensor("cbs", [2, 1024], BF16, kind="Internal").ap()

    A = nc.alloc_sbuf_tensor
    mask_sb = A("mask_sb", [128, 2048], BF16)
    lng_sb = A("lng_sb", [128, D], F32)
    ntv_sb = A("ntv_sb", [128, 16], F32)
    gascr = A("gascr", [128, 1024], BF16)
    tvec_sb = A("tvec_sb", [128, 16], F32)
    id2_sb = A("id2_sb", [128, 64], BF16)
    ones_sb = A("ones_sb", [1, 128], BF16)
    hind_sb = A("hind_sb", [128, 2], F32)
    hindT_sb = A("hindT_sb", [2, 128], F32)
    actb_sb = A("actb_sb", [128, 2], F32)
    gnwj_sb = A("gnwj_sb", [128, 8], F32)
    gnbj_sb = A("gnbj_sb", [128, 8], F32)
    wp1_sb = A("wp1_sb", [128, 8], F32)
    b1p_sb = A("b1p_sb", [128, 8], F32)
    w2p_sb = A("w2p_sb", [128, 8], F32)
    b1pp_sb = A("b1pp_sb", [128, 8], F32)
    bind_sb = A("bind_sb", [128, 1], F32)
    b2f_sb = A("b2f_sb", [64, 1], F32)
    b2b_sb = A("b2b_sb", [64, 1], BF16)
    cfm_sb = A("cfm_sb", [128, 8], F32)
    silf_sb = A("silf_sb", [128, 8], F32)
    silb_sb = A("silb_sb", [128, 8], BF16)
    bout_sb = A("bout_sb", [1, D], F32)
    bf_sb = A("bf_sb", [1, D], BF16)
    row2_sb = A("row2_sb", [1, D], BF16)
    mrow_sb = A("mrow_sb", [1, 512], BF16)
    cbT_sb = A("cbT_sb", [128, 8], F32)
    cbTb_sb = A("cbTb_sb", [128, 8], BF16)
    sbT_sb = A("sbT_sb", [128, 8], BF16)
    sbw_sb = A("sbw_sb", [128, 1], F32)

    wt = {n: A(f"wt_{n}", [128, NDC * D], BF16) for n in ("q", "k", "v", "o")}
    wmsl = A("wmsl", [128, NDC * 512], BF16)
    modt = A("modt", [128, 6 * D], BF16)
    windup_sb = A("windup", [128, NDC * 128], BF16)
    dg_sb = A("dg", [128, 8 * 64], BF16)
    wout_sb = A("wout_sb", [64, D], BF16)

    xa = [A(f"xa{i}", [128, D], F32) for i in range(2)]
    x2 = [A(f"x2_{i}", [128, D], F32) for i in range(2)]
    htl = [A(f"h{i}", [128, D], BF16) for i in range(2)]
    hT = [A(f"hT{i}", [128, D], BF16) for i in range(2)]
    tabq = [A(f"tabq{i}", [128, D], BF16) for i in range(2)]
    tabk = [A(f"tabk{i}", [128, D], BF16) for i in range(2)]
    qtm = [A(f"qtm{i}", [128, D], BF16) for i in range(2)]
    qfm = [A(f"qfm{i}", [128, D], BF16) for i in range(2)]
    ktm = [A(f"ktm{i}", [128, D], BF16) for i in range(2)]
    kfm = [A(f"kfm{i}", [128, D], BF16) for i in range(2)]
    vtm = [A(f"vtm{i}", [128, D], BF16) for i in range(2)]
    stl = [A(f"st{i}", [128, 2048], BF16) for i in range(2)]
    pbf = A("pbf", [128, 512], BF16)
    ybf = A("ybf", [128, NCH * D], BF16)
    y2s = A("y2s", [128, D], BF16)
    sqs = A("sqs", [128, D], BF16)
    gb1 = [A(f"gb1_{i}", [64, 128], BF16) for i in range(2)]

    xsum = A("xsum", [128, 16], F32)
    xsq = A("xsq", [128, 16], F32)
    mu = A("mu", [128, 16], F32)
    mu2 = A("mu2", [128, 16], F32)
    var = A("var", [128, 16], F32)
    rho = A("rho", [128, 16], F32)
    ysum_a = A("ysum_a", [128, 8], F32)
    ysq_a = A("ysq_a", [128, 8], F32)
    ysumc = A("ysumc", [128, 8], F32)
    ysqc = A("ysqc", [128, 8], F32)
    hstat = A("hstat", [128, 16], F32)
    hsc = A("hsc", [128, 32], F32)
    arep = A("arep", [128, 8], F32)
    brep = A("brep", [128, 8], F32)
    bbf = A("bbf", [128, 8], BF16)

    ps01 = nc.alloc_psum_tensor("ps01", [128, 1024], F32)
    psS2 = nc.alloc_psum_tensor("psS2", [128, 512], F32)
    psS3 = nc.alloc_psum_tensor("psS3", [128, 512], F32)
    ps4 = nc.alloc_psum_tensor("ps4", [128, 512], F32)
    psY = nc.alloc_psum_tensor("psY", [128, 1024], F32)
    ps7 = nc.alloc_psum_tensor("ps7", [128, 512], F32)

    sem_names = [
        "s_cst", "s_wc0", "s_wc1", "s_wc2", "s_wc3",
        "s_wtq", "s_wtk", "s_wtv", "s_wto",
        "s_xa0", "s_xa1", "s_wm0",
        "s_hT0", "s_hT1", "s_qT0", "s_qT1", "s_kT0", "s_kT1",
        "s_xr0", "s_xr1", "s_out", "vp",
        "t_gemv", "t_mod", "t_bf", "t_qkv", "t_S", "t_Y", "t_PU",
        "t_hs", "t_ab2", "t_r2", "t_A", "t_s", "t_g", "t_o", "s_cb", "t_sbw",
        "sc_silu", "sc_tab", "sc_sq", "sc_mu2", "sc_rho",
        "sc_ve", "sc_ye", "sc_y2", "sc_hp", "sc_g", "sc_hs",
    ]
    S = {n: nc.alloc_semaphore(n) for n in sem_names}
    NC_ALL = 16 * N_CONST_DMAS
    vpos = {}

    with nc.Block() as block:
        @block.vector
        def _(v):
            vn = [0]

            def vop(name, *a, mark=None, **kw):
                if vn[0] > 0:
                    v.wait_ge(S["vp"], vn[0])
                ins = getattr(v, name)(*a, **kw)
                vn[0] += 1
                ins.then_inc(S["vp"], 1)
                if mark is not None:
                    vpos[mark] = vn[0]
                return ins

            v.wait_ge(S["s_cst"], NC_ALL)
            vop("memset", ysum_a[:], 0.0)
            vop("memset", ysq_a[:], 0.0)
            v.wait_ge(S["sc_silu"], 1)
            vop("tensor_copy", silb_sb[:], silf_sb[:], mark="silu")

            def mod_step_v(s):
                j = SLICE_ORDER[s]
                v.wait_ge(S["t_gemv"], s + 1)
                if s >= 1:
                    v.wait_ge(S["t_mod"], s)
                vop("tensor_copy", mrow_sb[0:1, :], ps4[0:1, :], mark=("mrow", s))
                v.wait_ge(S["t_mod"], s + 1)
                vop("tensor_copy", modt[:, 512 * j:512 * (j + 1)], ps4[:, 0:512],
                    mark=("modp", s))

            def gate_adjust(off):
                gsl = modt[:, off:off + 1024]
                vop("tensor_single_scalar", gascr[:], gsl, 0.0, ALU.is_ge)
                vop("tensor_scalar", gascr[:], gascr[:], 2.0, -1.0, ALU.mult, ALU.add)
                vop("tensor_mul", gsl, gsl, gascr[:])
                vop("tensor_scalar_max", gsl, gsl, 0.25)
                vop("tensor_mul", gsl, gsl, gascr[:])

            for s_i in range(4):
                mod_step_v(s_i)
            vop("tensor_scalar_add", modt[:, 1024:2048], modt[:, 1024:2048], 1.0)

            def v_pre(c):
                p = c % 2
                v.wait_ge(S[f"s_xa{p}"], 16 * (c // 2 + 1))
                v.wait_ge(S["sc_sq"], c + 1)
                vop("tensor_scalar", sqs[:], sqs[:], 1.0, None, ALU.mult, ALU.add,
                    accum_out=xsq[:, c:c + 1])
                vop("tensor_scalar", sqs[:], xa[p][:], 1.0, None, ALU.mult, ALU.add,
                    accum_out=xsum[:, c:c + 1])
                vop("tensor_scalar_mul", mu[:, c:c + 1], xsum[:, c:c + 1], 1.0 / D,
                    mark=("mu", c))
                vop("tensor_mul", mu2[:, c:c + 1], mu[:, c:c + 1], mu[:, c:c + 1])
                vop("scalar_tensor_tensor", var[:, c:c + 1], xsq[:, c:c + 1], 1.0 / D,
                    mu2[:, c:c + 1], ALU.mult, ALU.subtract, mark=("var", c))
                v.wait_ge(S["sc_rho"], c + 1)
                vop("reciprocal", rho[:, c:c + 1], rho[:, c:c + 1])
                if c >= 2:
                    v.wait_ge(S[f"s_hT{p}"], 16 * (c // 2))
                vop("tensor_scalar", htl[p][:], xa[p][:], mu[:, c:c + 1], rho[:, c:c + 1],
                    ALU.subtract, ALU.mult, mark=("hA", c))
                vop("tensor_mul", htl[p][:], htl[p][:], modt[:, 1024:2048])
                vop("tensor_add", htl[p][:], htl[p][:], modt[:, 0:1024], mark=("h", c))

            v_pre(0)
            v_pre(1)

            for c in range(KB_NCH):
                p = c % 2
                if 0 <= c <= 7:
                    mod_step_v(c + 4)
                v.wait_ge(S["sc_tab"], c + 1)
                if c >= 2:
                    v.wait_ge(S[f"s_qT{p}"], 16 * (c // 2))
                    v.wait_ge(S[f"s_kT{p}"], 16 * (c // 2))
                    v.wait_ge(S["t_PU"], c - 1)
                v.wait_ge(S["t_qkv"], 6 * c + 1)
                vop("tensor_mul", qtm[p][:, 0:512], ps01[:, 0:512], tabq[p][:, 0:512],
                    mark=("qe", c, 0))
                v.wait_ge(S["t_qkv"], 6 * c + 2)
                vop("tensor_mul", qtm[p][:, 512:1024], ps01[:, 512:1024],
                    tabq[p][:, 512:1024], mark=("qe", c, 1))
                v.wait_ge(S["t_qkv"], 6 * c + 3)
                vop("tensor_mul", ktm[p][:, 0:512], ps01[:, 0:512], tabk[p][:, 0:512],
                    mark=("qe", c, 2))
                if c >= 1:
                    v.wait_ge(S["sc_ye"], c)
                    vop("reduce_sum", ysumc[:],
                        ybf[:, 1024 * (c - 1):1024 * c].rearrange("p (j a) -> p j a", j=8),
                        mybir.AxisListType.X)
                    vop("tensor_add", ysum_a[:], ysum_a[:], ysumc[:])
                v.wait_ge(S["t_qkv"], 6 * c + 4)
                vop("tensor_mul", ktm[p][:, 512:1024], ps01[:, 512:1024],
                    tabk[p][:, 512:1024], mark=("qke", c))
                if c >= 1:
                    v.wait_ge(S["sc_y2"], c)
                    vop("reduce_sum", ysqc[:], y2s[:].rearrange("p (j a) -> p j a", j=8),
                        mybir.AxisListType.X)
                    vop("tensor_add", ysq_a[:], ysq_a[:], ysqc[:], mark=("yst", c - 1))
                    v.wait_ge(S["t_PU"], c)
                    vop("tensor_single_scalar", pbf[:], ps7[:], 0.125,
                        ALU.mult, mark=("Pe", c - 1))
                if c >= 1:
                    v.wait_ge(S["t_Y"], c)
                v.wait_ge(S["t_S"], 4 * c + 1)
                vop("tensor_mul", stl[p][:, 0:512], psS2[:], mask_sb[:, 0:512],
                    mark=("Se", c, 0))
                v.wait_ge(S["t_S"], 4 * c + 2)
                vop("tensor_mul", stl[p][:, 1024:1536], psS3[:], mask_sb[:, 1024:1536],
                    mark=("Se", c, 1))
                v.wait_ge(S["t_S"], 4 * c + 3)
                vop("tensor_mul", stl[p][:, 512:1024], psS2[:], mask_sb[:, 512:1024],
                    mark=("Se", c, 2))
                v.wait_ge(S["t_S"], 4 * c + 4)
                vop("tensor_mul", stl[p][:, 1536:2048], psS3[:], mask_sb[:, 1536:2048],
                    mark=("Se", c))
                if c == 2:
                    gate_adjust(2048)
                if c == 3:
                    v.wait_ge(S["s_wto"], 16 * 8)
                    for dj in range(8):
                        vop("tensor_mul", wt["o"][:, 1024 * dj:1024 * (dj + 1)],
                            wt["o"][:, 1024 * dj:1024 * (dj + 1)],
                            modt[:, 2048:3072], mark=("wog" if dj == 7 else None))
                if c == 4:
                    gate_adjust(5120)
                if c == 5:
                    vop("tensor_scalar_mul", b1pp_sb[:], wp1_sb[:], bind_sb[:, 0:1])
                    vop("tensor_add", b1pp_sb[:], b1pp_sb[:], b1p_sb[:])
                    for jj2 in range(8):
                        vop("tensor_scalar_mul", dg_sb[:, 64 * jj2:64 * (jj2 + 1)],
                            id2_sb[:], w2p_sb[:, jj2:jj2 + 1])
                    vop("tensor_copy", b2b_sb[:], b2f_sb[:])
                    vop("tensor_mul", wout_sb[0:64, :], wout_sb[0:64, :],
                        modt[0:64, 5120:6144])
                    vop("tensor_mul", bout_sb[0:1, :], bout_sb[0:1, :],
                        modt[0:1, 5120:6144], mark="ffn1")
                if c == 6:
                    v.wait_ge(S["t_bf"], 1)
                    vop("tensor_add", bf_sb[0:1, 0:512], ps4[0:1, 0:512],
                        bout_sb[0:1, 0:512], mark="bfh0")
                if c == 7:
                    v.wait_ge(S["t_bf"], 2)
                    vop("tensor_add", bf_sb[0:1, 512:1024], ps4[0:1, 0:512],
                        bout_sb[0:1, 512:1024], mark="ffn2")
                if c == 8:
                    vop("tensor_scalar_add", modt[:, 4096:5120], modt[:, 4096:5120], 1.0,
                        mark="cb1")
                if c + 2 < KB_NCH:
                    v_pre(c + 2)

            v.wait_ge(S["sc_ye"], NCH)
            vop("reduce_sum", ysumc[:],
                ybf[:, 1024 * (NCH - 1):1024 * NCH].rearrange("p (j a) -> p j a", j=8),
                mybir.AxisListType.X)
            vop("tensor_add", ysum_a[:], ysum_a[:], ysumc[:])
            v.wait_ge(S["sc_y2"], NCH)
            vop("reduce_sum", ysqc[:], y2s[:].rearrange("p (j a) -> p j a", j=8),
                mybir.AxisListType.X)
            vop("tensor_add", ysq_a[:], ysq_a[:], ysqc[:], mark=("yst", NCH - 1))
            v.wait_ge(S["t_PU"], NCH)
            vop("tensor_single_scalar", pbf[:], ps7[:], 0.125,
                ALU.mult, mark=("Pe", NCH - 1))

            if not KB_PH3:
                return
            v.wait_ge(S["t_hs"], 1)
            vop("tensor_copy", hstat[0:2, :], ps4[0:2, 0:16])
            vop("tensor_scalar_mul", hsc[0:2, 0:8], hstat[0:2, 0:8], 1.0 / (DH * T),
                mark="mu2b")
            vop("tensor_mul", hsc[0:2, 24:32], hsc[0:2, 0:8], hsc[0:2, 0:8])
            vop("scalar_tensor_tensor", hsc[0:2, 16:24], hstat[0:2, 8:16], 1.0 / (DH * T),
                hsc[0:2, 24:32], ALU.mult, ALU.subtract, mark="var2b")
            v.wait_ge(S["sc_hs"], 1)
            vop("reciprocal", hsc[0:2, 24:32], hsc[0:2, 24:32])
            vop("tensor_mul", hsc[0:2, 8:16], hsc[0:2, 0:8], hsc[0:2, 24:32])
            vop("tensor_scalar_mul", hsc[0:2, 8:16], hsc[0:2, 8:16], -1.0, mark="ab")
            v.wait_ge(S["t_ab2"], 1)
            vop("tensor_copy", arep[:], ps4[:, 0:8])
            vop("tensor_copy", brep[:], ps4[:, 8:16])
            vop("tensor_mul", arep[:], arep[:], gnwj_sb[:])
            vop("tensor_mul", brep[:], brep[:], gnwj_sb[:])
            vop("tensor_add", brep[:], brep[:], gnbj_sb[:])
            vop("tensor_copy", bbf[:], brep[:], mark="AB")
            v.wait_ge(S["t_r2"], 1)
            for dj in range(8):
                vop("tensor_scalar_mul", wt["o"][:, 1024 * dj:1024 * (dj + 1)],
                    wt["o"][:, 1024 * dj:1024 * (dj + 1)], arep[:, dj:dj + 1])
            vop("tensor_copy", row2_sb[0:1, 0:512], ps4[0:1, 0:512])
            vop("tensor_copy", row2_sb[0:1, 512:1024], ps7[0:1, 0:512], mark="woa")
            v.wait_ge(S["t_sbw"], 1)
            vop("tensor_copy", sbw_sb[:], psS3[:, 0:1])
            vop("tensor_scalar", arep[:], wp1_sb[:], sbw_sb[:, 0:1], None,
                ALU.mult, ALU.bypass)
            vop("tensor_add", b1pp_sb[:], b1pp_sb[:], arep[:], mark="b1adj")
            v.wait_ge(S["s_cb"], 48)
            vop("tensor_copy", cbT_sb[:], cbTb_sb[:])
            for dc in range(NDC):
                vop("tensor_scalar_mul", windup_sb[:, 128 * dc:128 * (dc + 1)],
                    windup_sb[:, 128 * dc:128 * (dc + 1)], cbT_sb[:, dc:dc + 1],
                    mark=("wfold" if dc == NDC - 1 else None))

            def v3_pre(tau):
                p = tau % 2
                v.wait_ge(S[f"s_xr{p}"], 16 * (tau // 2 + 1))
                v.wait_ge(S["t_A"], 2 * tau + 1)
                vop("scalar_tensor_tensor", x2[p][:, 0:512], ps01[:, 0:512], 1.0,
                    xa[p][:, 0:512], ALU.mult, ALU.add, mark=("x2h", tau, 0))
                v.wait_ge(S["t_A"], 2 * tau + 2)
                vop("scalar_tensor_tensor", x2[p][:, 512:1024], ps01[:, 512:1024], 1.0,
                    xa[p][:, 512:1024], ALU.mult, ALU.add, mark=("x2h", tau, 1))
                vpos[("x2", tau)] = vpos[("x2h", tau, 1)]
                v.wait_ge(S["sc_sq"], 17 + tau)
                vop("tensor_scalar", sqs[:], sqs[:], 1.0, None, ALU.mult, ALU.add,
                    accum_out=xsq[:, tau:tau + 1])
                vop("tensor_scalar", sqs[:], x2[p][:], 1.0, None, ALU.mult, ALU.add,
                    accum_out=xsum[:, tau:tau + 1])
                vop("tensor_scalar_mul", mu[:, tau:tau + 1], xsum[:, tau:tau + 1],
                    1.0 / D, mark=("mu", NCH + tau))
                vop("tensor_mul", mu2[:, tau:tau + 1], mu[:, tau:tau + 1],
                    mu[:, tau:tau + 1])
                vop("scalar_tensor_tensor", var[:, tau:tau + 1], xsq[:, tau:tau + 1],
                    1.0 / D, mu2[:, tau:tau + 1], ALU.mult, ALU.subtract,
                    mark=("var", NCH + tau))
                v.wait_ge(S["sc_rho"], 17 + tau)
                vop("reciprocal", rho[:, tau:tau + 1], rho[:, tau:tau + 1])
                v.wait_ge(S[f"s_hT{p}"], 16 * (8 + tau // 2))
                vop("tensor_scalar", htl[p][:], x2[p][:], mu[:, tau:tau + 1],
                    rho[:, tau:tau + 1], ALU.subtract, ALU.mult, mark=("hA", NCH + tau))
                vpos[("h", NCH + tau)] = vpos[("hA", NCH + tau)]

            v3_pre(0)
            v3_pre(1)
            for tau in range(NCH):
                p = tau % 2
                v.wait_ge(S["t_o"], tau + 1)
                vop("scalar_tensor_tensor", xa[p][:], psY[:], 1.0, x2[p][:],
                    ALU.mult, ALU.add, mark=("out", tau))
                if tau + 2 < NCH:
                    v3_pre(tau + 2)

        @block.sync
        def _(s):
            def chain():
                pass

            for i, nm in enumerate(("q", "k", "v", "o")):
                s.wait_ge(S[f"s_wc{i}"], 16)
                for oc in range(8):
                    dst = wt[nm][:].rearrange("p (dj oc b) -> p dj oc b", dj=8, oc=8)[:, :, oc, :]
                    s.dma_start_transpose(dst, wsc[i][128 * oc:128 * (oc + 1), :]).then_inc(
                        S[f"s_wt{nm}"], 16)
            for c in range(KB_NCH):
                p = c % 2
                if c == 10:
                    s.wait_ge(S["vp"], vpos["cb1"])
                    s.dma_start(cbs_d[0:1, :], modt[0:1, 4096:5120]).then_inc(S["s_cb"], 16)
                    s.wait_ge(S["s_cb"], 16)
                    s.dma_start(cbs_d[1:2, :], modt[0:1, 3072:4096]).then_inc(S["s_cb"], 16)
                    s.wait_ge(S["s_cb"], 32)
                    s.dma_start_transpose(
                        cbTb_sb[:].rearrange("p (j b) -> p j b", j=8),
                        cbs_d[0:1, :],
                    ).then_inc(S["s_cb"], 16)
                    s.wait_ge(S["s_cb"], 48)
                    s.dma_start_transpose(
                        sbT_sb[:].rearrange("p (j b) -> p j b", j=8),
                        cbs_d[1:2, :],
                    ).then_inc(S["s_cb"], 16)
                s.wait_ge(S["vp"], vpos[("h", c)])
                if c >= 2:
                    s.wait_ge(S["t_qkv"], 6 * (c - 1))
                s.dma_start_transpose(
                    hT[p][:].rearrange("p (dj b) -> p dj b", dj=8), htl[p][:]
                ).then_inc(S[f"s_hT{p}"], 16)
                if KB_STAGE < 2:
                    continue
                s.wait_ge(S["vp"], vpos[("qe", c, 1)])
                if c >= 2:
                    s.wait_ge(S["t_Y"], c - 1)
                s.dma_start_transpose(
                    qfm[p][:].rearrange("p (dj b) -> p dj b", dj=8), qtm[p][:]
                ).then_inc(S[f"s_qT{p}"], 16)
                s.wait_ge(S["vp"], vpos[("qke", c)])
                s.dma_start_transpose(
                    kfm[p][:].rearrange("p (dj b) -> p dj b", dj=8), ktm[p][:]
                ).then_inc(S[f"s_kT{p}"], 16)
            if not KB_PH3:
                return
            for tau in range(NCH):
                p = tau % 2
                s.wait_ge(S["vp"], vpos[("h", NCH + tau)])
                if tau >= 2:
                    s.wait_ge(S["t_s"], tau - 1)
                else:
                    s.wait_ge(S["t_qkv"], 6 * (15 + tau))
                s.dma_start_transpose(
                    hT[p][:].rearrange("p (dj b) -> p dj b", dj=8), htl[p][:]
                ).then_inc(S[f"s_hT{p}"], 16)
                s.wait_ge(S["vp"], vpos[("out", tau)])
                if tau >= 1:
                    s.wait_ge(S["s_out"], 16 * tau)
                s.dma_start(out_d[128 * tau:128 * (tau + 1), :], xa[p][:]).then_inc(
                    S["s_out"], 16)
            s.wait_ge(S["s_out"], 16 * NCH)

        @block.tensor
        def _(t):
            t.wait_ge(S["s_cst"], NC_ALL)
            t.wait_ge(S["vp"], vpos["silu"])

            def mod_step_t(s):
                t.wait_ge(S["s_wm0"], 16 * (s + 1))
                if s >= 1:
                    t.wait_ge(S["vp"], vpos[("modp", s - 1)])
                if s == 11:
                    t.wait_ge(S["vp"], vpos["bfh0"])
                for dc in range(NDC):
                    mm2 = t.matmul(ps4[0:1, :], silb_sb[:, dc:dc + 1],
                                   wmsl[:, 512 * dc:512 * (dc + 1)],
                                   start=(dc == 0), stop=(dc == NDC - 1))
                mm2.then_inc(S["t_gemv"], 1)
                t.wait_ge(S["vp"], vpos[("mrow", s)])
                t.matmul(ps4[:, 0:512], ones_sb[0:1, :], mrow_sb[0:1, :],
                         start=True, stop=True).then_inc(S["t_mod"], 1)

            for s_i in range(4):
                mod_step_t(s_i)

            for c in range(KB_NCH):
                p = c % 2
                if 0 <= c <= 7:
                    mod_step_t(c + 4)
                if c == 6:
                    t.wait_ge(S["vp"], vpos["ffn1"])
                    t.wait_ge(S["vp"], vpos[("modp", 10)])
                    t.matmul(ps4[0:1, 0:512], b2b_sb[:, 0:1],
                             wout_sb[0:64, 0:512], start=True,
                             stop=True).then_inc(S["t_bf"], 1)
                if c == 7:
                    t.wait_ge(S["vp"], vpos["bfh0"])
                    t.wait_ge(S["vp"], vpos[("modp", 11)])
                    t.matmul(ps4[0:1, 0:512], b2b_sb[:, 0:1],
                             wout_sb[0:64, 512:1024], start=True,
                             stop=True).then_inc(S["t_bf"], 1)
                if c == 0:
                    for nm in ("q", "k", "v"):
                        t.wait_ge(S[f"s_wt{nm}"], 16 * 8)
                t.wait_ge(S[f"s_hT{p}"], 16 * (c // 2 + 1))
                for g, (nm, hf) in enumerate(
                        (("q", 0), ("q", 1), ("k", 0), ("k", 1), ("v", 0), ("v", 1))):
                    if g == 0:
                        if c >= 1:
                            t.wait_ge(S["sc_ve"], 2 * c - 1)
                    elif g == 1:
                        if c >= 1:
                            t.wait_ge(S["sc_ve"], 2 * c)
                    elif g == 5:
                        t.wait_ge(S["vp"], vpos[("qke", c)])
                    else:
                        t.wait_ge(S["vp"], vpos[("qe", c, g - 2)])
                    for dc in range(NDC):
                        mm = t.matmul(
                            ps01[:, 512 * hf:512 * (hf + 1)],
                            hT[p][:, 128 * dc:128 * (dc + 1)],
                            wt[nm][:, 1024 * dc + 512 * hf:1024 * dc + 512 * (hf + 1)],
                            start=(dc == 0), stop=(dc == NDC - 1))
                    mm.then_inc(S["t_qkv"], 1)

                if KB_STAGE < 3:
                    continue
                t.wait_ge(S[f"s_qT{p}"], 16 * (c // 2 + 1))
                t.wait_ge(S[f"s_kT{p}"], 16 * (c // 2 + 1))
                for pss in range(4):
                    gg = pss % 2
                    half = pss // 2
                    psS = psS2 if gg == 0 else psS3
                    if half == 1:
                        t.wait_ge(S["vp"], vpos[("Se", c, pss - 2)])
                    elif c >= 1:
                        t.wait_ge(S["vp"],
                                  vpos[("Se", c - 1, 2)] if gg == 0
                                  else vpos[("Se", c - 1)])
                    for r in range(4):
                        jj = 4 * half + r
                        tp = (64 * gg, 0) if gg else None
                        mm = t.matmul(
                            psS[:, 128 * r:128 * (r + 1)],
                            kfm[p][64 * gg:64 * (gg + 1), 128 * jj:128 * (jj + 1)],
                            qfm[p][64 * gg:64 * (gg + 1), 128 * jj:128 * (jj + 1)],
                            start=(r == 0), stop=(r == 3), tile_position=tp)
                    mm.then_inc(S["t_S"], 1)

                if KB_STAGE < 5:
                    continue
                started = {}
                if c > 0:
                    t.wait_ge(S["vp"], vpos[("Pe", c - 1)])
                    t.wait_ge(S["sc_y2"], c)
                t.wait_ge(S["vp"], vpos[("Se", c)])
                t.wait_ge(S["sc_ve"], 2 * c + 2)
                if c > 0:
                    for h in [2 * j for j in range(8)] + [2 * j + 1 for j in range(8)]:
                        jj, gg = h // 2, h % 2
                        bank = 4 + jj // 4
                        st = not started.get((bank, gg), False)
                        started[(bank, gg)] = True
                        tp = (64 * gg, 64 * gg) if gg else None
                        t.matmul(psY[64 * gg:64 * (gg + 1), 128 * jj:128 * (jj + 1)],
                                 pbf[64 * gg:64 * (gg + 1), 64 * jj:64 * (jj + 1)],
                                 qfm[p][64 * gg:64 * (gg + 1), 128 * jj:128 * (jj + 1)],
                                 start=st, stop=False, tile_position=tp,
                                 skip_group_check=True)
                for h in range(H):
                    jj, gg = h // 2, h % 2
                    bank = 4 + jj // 4
                    st = not started.get((bank, gg), False)
                    started[(bank, gg)] = True
                    tp = (0, 64 * gg) if gg else None
                    scol = 1024 * gg + 128 * jj
                    mm = t.matmul(psY[64 * gg:64 * (gg + 1), 128 * jj:128 * (jj + 1)],
                                  vtm[p][:, 64 * h:64 * (h + 1)],
                                  stl[p][:, scol:scol + 128],
                                  start=st, stop=(jj % 4 == 3), tile_position=tp,
                                  skip_group_check=True)
                mm.then_inc(S["t_Y"], 1)
                if KB_STAGE < 6:
                    continue
                for h in range(H):
                    jj, gg = h // 2, h % 2
                    tp = (0, 64 * gg) if gg else None
                    mm = t.matmul(ps7[64 * gg:64 * (gg + 1), 64 * jj:64 * (jj + 1)],
                                  ktm[p][:, 64 * h:64 * (h + 1)],
                                  vtm[p][:, 64 * h:64 * (h + 1)],
                                  start=(c == 0 and jj == 0),
                                  stop=(c == NCH - 1 and jj == 7), tile_position=tp,
                                  skip_group_check=True)
                mm.then_inc(S["t_PU"], 1)

            if not KB_PH3:
                return
            t.wait_ge(S["vp"], vpos[("yst", NCH - 1)])
            t.matmul(ps4[0:2, 0:8], hind_sb[:], ysum_a[:], start=True, stop=False)
            t.matmul(ps4[0:2, 8:16], hind_sb[:], ysq_a[:], start=False,
                     stop=True).then_inc(S["t_hs"], 1)
            t.wait_ge(S["vp"], vpos["ab"])
            t.matmul(ps4[:, 0:8], hindT_sb[:], hsc[0:2, 24:32], start=True, stop=False)
            t.matmul(ps4[:, 8:16], hindT_sb[:], hsc[0:2, 8:16], start=False,
                     stop=True).then_inc(S["t_ab2"], 1)
            t.wait_ge(S["vp"], vpos["AB"])
            t.wait_ge(S["vp"], vpos["wog"])
            t.wait_ge(S["vp"], vpos[("Pe", NCH - 1)])
            for hf in range(2):
                psr = ps4 if hf == 0 else ps7
                for dj in range(8):
                    mm = t.matmul(psr[0:1, 0:512], bbf[:, dj:dj + 1],
                                  wt["o"][:, 1024 * dj + 512 * hf:1024 * dj + 512 * (hf + 1)],
                                  start=(dj == 0), stop=(dj == 7))
            mm.then_inc(S["t_r2"], 1)

            t.wait_ge(S["vp"], vpos["woa"])
            t.wait_ge(S["sc_ye"], NCH)
            t.wait_ge(S["vp"], vpos[("Se", NCH - 1)])
            t.wait_ge(S["s_cb"], 64)
            for dc in range(NDC):
                mm = t.matmul(psS3[:, 0:1], windup_sb[:, 128 * dc:128 * (dc + 1)],
                              sbT_sb[:, dc:dc + 1], start=(dc == 0), stop=(dc == NDC - 1))
            mm.then_inc(S["t_sbw"], 1)
            def p3_A(tau):
                for hf in range(2):
                    if tau >= 1:
                        t.wait_ge(S["vp"], vpos[("x2h", tau - 1, hf)])
                    for dj in range(8):
                        t.matmul(ps01[:, 512 * hf:512 * (hf + 1)],
                                 ybf[:, 1024 * tau + 128 * dj:1024 * tau + 128 * (dj + 1)],
                                 wt["o"][:, 1024 * dj + 512 * hf:1024 * dj + 512 * (hf + 1)],
                                 start=(dj == 0), stop=False)
                    mm = t.matmul(ps01[:, 512 * hf:512 * (hf + 1)], ones_sb[0:1, :],
                                  row2_sb[0:1, 512 * hf:512 * (hf + 1)], start=False, stop=True)
                    mm.then_inc(S["t_A"], 1)

            def p3_s(tau):
                p = tau % 2
                t.wait_ge(S[f"s_hT{p}"], 16 * (8 + tau // 2 + 1))
                if tau == 0:
                    t.wait_ge(S["vp"], vpos["wfold"])
                if tau >= 1:
                    t.wait_ge(S["sc_hp"], tau)
                for dc in range(NDC):
                    mm = t.matmul(psS2[:, 0:128], windup_sb[:, 128 * dc:128 * (dc + 1)],
                                  hT[p][:, 128 * dc:128 * (dc + 1)],
                                  start=(dc == 0), stop=(dc == NDC - 1))
                mm.then_inc(S["t_s"], 1)

            def p3_go(tau):
                p = tau % 2
                t.wait_ge(S["sc_hp"], tau + 1)
                if tau >= 1:
                    t.wait_ge(S["sc_g"], tau)
                for j in range(8):
                    mm = t.matmul(psS3[0:64, 0:128], dg_sb[:, 64 * j:64 * (j + 1)],
                                  qtm[p][:, 128 * j:128 * (j + 1)],
                                  start=(j == 0), stop=(j == 7))
                mm.then_inc(S["t_g"], 1)
                t.wait_ge(S["sc_g"], tau + 1)
                if tau >= 1:
                    t.wait_ge(S["vp"], vpos[("out", tau - 1)])
                for hf in range(2):
                    t.matmul(psY[:, 512 * hf:512 * (hf + 1)], gb1[p][:],
                             wout_sb[0:64, 512 * hf:512 * (hf + 1)], start=True, stop=False)
                    mm = t.matmul(psY[:, 512 * hf:512 * (hf + 1)], ones_sb[0:1, :],
                                  bf_sb[0:1, 512 * hf:512 * (hf + 1)], start=False, stop=True)
                mm.then_inc(S["t_o"], 1)

            p3_A(0)
            p3_A(1)
            for tau in range(NCH):
                p3_s(tau)
                p3_go(tau)
                if tau + 2 < NCH:
                    p3_A(tau + 2)

        @block.scalar
        def _(sc):
            sc.wait_ge(S["s_cst"], NC_ALL)
            sc.activation(silf_sb[:], cfm_sb[:], AF.Silu).then_inc(S["sc_silu"], 1)

            def sc_tab(c):
                p = c % 2
                if c >= 2:
                    sc.wait_ge(S["vp"], vpos[("qke", c - 2)])
                sc.activation(tabq[p][:], lng_sb[:], AF.Exp,
                              bias=0.0, scale=tvec_sb[:, c:c + 1])
                sc.activation(tabk[p][:], lng_sb[:], AF.Exp,
                              bias=0.0, scale=ntv_sb[:, c:c + 1]).then_inc(S["sc_tab"], 1)

            def sc_sq(c):
                p = c % 2
                sc.wait_ge(S[f"s_xa{p}"], 16 * (c // 2 + 1))
                if c >= 1:
                    sc.wait_ge(S["vp"], vpos[("mu", c - 1)])
                sc.square(sqs[:], xa[p][:]).then_inc(S["sc_sq"], 1)

            def sc_rho(c):
                sc.wait_ge(S["vp"], vpos[("var", c)])
                sc.activation(rho[:, c:c + 1], var[:, c:c + 1], AF.Sqrt,
                              bias=actb_sb[:, 1:2], scale=1.0).then_inc(S["sc_rho"], 1)

            sc_tab(0)
            sc_sq(0)
            sc_rho(0)
            sc_tab(1)
            sc_sq(1)
            sc_rho(1)
            for c in range(KB_NCH):
                p = c % 2
                if c + 2 < KB_NCH:
                    sc_sq(c + 2)
                    sc_tab(c + 2)
                sc.wait_ge(S["t_qkv"], 6 * c + 5)
                if c >= 2:
                    sc.wait_ge(S["t_PU"], c - 1)
                sc.copy(vtm[p][:, 0:512], ps01[:, 0:512]).then_inc(S["sc_ve"], 1)
                sc.wait_ge(S["t_qkv"], 6 * c + 6)
                sc.copy(vtm[p][:, 512:1024], ps01[:, 512:1024]).then_inc(S["sc_ve"], 1)
                if c + 2 < KB_NCH:
                    sc_rho(c + 2)
                sc.wait_ge(S["t_Y"], c + 1)
                sc.copy(ybf[:, 1024 * c:1024 * (c + 1)], psY[:]).then_inc(S["sc_ye"], 1)
                if c >= 1:
                    sc.wait_ge(S["vp"], vpos[("yst", c - 1)])
                sc.square(y2s[:], psY[:]).then_inc(S["sc_y2"], 1)
            if not KB_PH3:
                return
            sc.wait_ge(S["vp"], vpos["var2b"])
            sc.activation(hsc[0:2, 24:32], hsc[0:2, 16:24], AF.Sqrt,
                          bias=actb_sb[0:2, 1:2], scale=1.0).then_inc(S["sc_hs"], 1)
            for tau in range(NCH):
                p = tau % 2
                sc.wait_ge(S["vp"], vpos[("x2", tau)])
                if tau == 0:
                    sc.wait_ge(S["vp"], vpos["mu2b"])
                else:
                    sc.wait_ge(S["vp"], vpos[("mu", NCH + tau - 1)])
                sc.square(sqs[:], x2[p][:]).then_inc(S["sc_sq"], 1)
                sc.wait_ge(S["vp"], vpos[("var", NCH + tau)])
                sc.activation(rho[:, tau:tau + 1], var[:, tau:tau + 1], AF.Sqrt,
                              bias=actb_sb[:, 1:2], scale=1.0).then_inc(S["sc_rho"], 1)
                sc.wait_ge(S["t_s"], tau + 1)
                if tau == 0:
                    sc.wait_ge(S["vp"], vpos["b1adj"])
                if tau >= 2:
                    sc.wait_ge(S["t_g"], tau - 1)
                else:
                    sc.wait_ge(S[f"s_qT{p}"], 16 * 8)
                for j in range(8):
                    act = sc.activation(qtm[p][:, 128 * j:128 * (j + 1)], psS2[:, 0:128],
                                        AF.Relu, bias=b1pp_sb[:, j:j + 1],
                                        scale=wp1_sb[:, j:j + 1])
                act.then_inc(S["sc_hp"], 1)
                sc.wait_ge(S["t_g"], tau + 1)
                if tau >= 2:
                    sc.wait_ge(S["t_o"], tau - 1)
                sc.copy(gb1[p][:], psS3[0:64, 0:128]).then_inc(S["sc_g"], 1)

        @block.gpsimd
        def _(g):
            g.dma_start(mask_sb[:], mask_d[:, :]).then_inc(S["s_cst"], 16)
            g.dma_start(lng_sb[:], lng_d[:, :].to_broadcast((128, D))).then_inc(S["s_cst"], 16)
            g.dma_start(ntv_sb[:], ntv_d[:, :]).then_inc(S["s_cst"], 16)
            g.dma_start(tvec_sb[:], tvec_d[:, :]).then_inc(S["s_cst"], 16)
            g.dma_start(id2_sb[:], id2_d[:, :]).then_inc(S["s_cst"], 16)
            g.dma_start(ones_sb[:], ones1_d[:, :]).then_inc(S["s_cst"], 16)
            g.dma_start(hind_sb[:], hind_d[:, :]).then_inc(S["s_cst"], 16)
            g.dma_start(hindT_sb[:], hindT_d[:, :]).then_inc(S["s_cst"], 16)
            g.dma_start(actb_sb[:], actb_d[:, :]).then_inc(S["s_cst"], 16)
            g.dma_start(gnwj_sb[:], gnw_d[:, :]).then_inc(S["s_cst"], 16)
            g.dma_start(gnbj_sb[:], gnb_d[:, :]).then_inc(S["s_cst"], 16)
            g.dma_start(wp1_sb[:], w1p_d[:, :]).then_inc(S["s_cst"], 16)
            g.dma_start(b1p_sb[:], b1p_d[:, :]).then_inc(S["s_cst"], 16)
            g.dma_start(w2p_sb[:], w2p_d[:, :]).then_inc(S["s_cst"], 16)
            g.dma_start(bind_sb[0:64, :], bin_d[:, :]).then_inc(S["s_cst"], 16)
            g.dma_start(bind_sb[64:128, :], bin_d[:, :]).then_inc(S["s_cst"], 16)
            g.dma_start(b2f_sb[:], b2_d[:, :]).then_inc(S["s_cst"], 16)
            g.dma_start(bout_sb[:], bout_d[:, :]).then_inc(S["s_cst"], 16)
            g.dma_start(cfm_sb[:], crow_d[:, :]).then_inc(S["s_cst"], 16)
            g.dma_start(wout_sb[:], wout_d[:, :]).then_inc(S["s_cst"], 16)
            g.dma_start(
                windup_sb[:].rearrange("p (c m) -> p c m", c=NDC)[:, :, 0:64],
                win_d[:, :].rearrange("(c p) m -> p c m", c=NDC),
            ).then_inc(S["s_cst"], 16)
            g.dma_start(
                windup_sb[:].rearrange("p (c m) -> p c m", c=NDC)[:, :, 64:128],
                win_d[:, :].rearrange("(c p) m -> p c m", c=NDC),
            ).then_inc(S["s_cst"], 16)

            for i, wd in enumerate((wq_d, wk_d, wv_d, wo_d)):
                g.dma_start(wsc[i][:, :], wd[:, :]).then_inc(S[f"s_wc{i}"], 16)

            def wm_load(s):
                j = SLICE_ORDER[s]
                if s >= 1:
                    g.wait_ge(S["t_gemv"], s)
                g.dma_start(
                    wmsl[:].rearrange("p (c o) -> p c o", c=NDC),
                    wmod_d[:, :].rearrange("(c p) o -> p c o", c=NDC)[:, :, 512 * j:512 * (j + 1)],
                ).then_inc(S["s_wm0"], 16)

            def x_load(c):
                if c >= 2:
                    g.wait_ge(S["vp"], vpos[("hA", c - 2)])
                g.dma_start(xa[c % 2][:], x_d[128 * c:128 * (c + 1), :]).then_inc(
                    S[f"s_xa{c % 2}"], 16)

            wm_load(0); wm_load(1)
            x_load(0)
            wm_load(2); wm_load(3)
            x_load(1)
            for c in range(2, NCH):
                if c + 2 < 12:
                    wm_load(c + 2)
                x_load(c)

            if KB_PH3:
                for tau in range(NCH):
                    if tau <= 1:
                        g.wait_ge(S["vp"], vpos[("hA", 14 + tau)])
                    else:
                        g.wait_ge(S["s_out"], 16 * (tau - 1))
                    g.dma_start(xa[tau % 2][:], x_d[128 * tau:128 * (tau + 1), :]).then_inc(
                        S[f"s_xr{tau % 2}"], 16)
                g.wait_ge(S["s_out"], 16 * NCH)

    return nc


_NC_CACHE = None


def kernel(L0_attn_x, L0_attn_W_q, L0_attn_W_k, L0_attn_W_v, L0_attn_W_o,
           L0_attn_gn_weight, L0_attn_gn_bias, L0_ffn_Win, L0_ffn_bin_,
           L0_ffn_W1, L0_ffn_b1, L0_ffn_W2, L0_ffn_b2, L0_ffn_Wout,
           L0_ffn_bout, L0_c, L0_wmod):
    global _NC_CACHE
    if _NC_CACHE is None:
        _NC_CACHE = build_nc()
    nc = _NC_CACHE

    f = np.float32
    x = np.ascontiguousarray(np.asarray(L0_attn_x, dtype=f))
    cmat = np.ascontiguousarray(np.asarray(L0_c, dtype=f))
    W1 = np.asarray(L0_ffn_W1, dtype=f)
    b1 = np.asarray(L0_ffn_b1, dtype=f)
    W2 = np.asarray(L0_ffn_W2, dtype=f)
    pair = lambda M: np.ascontiguousarray(
        M.reshape(64, 8, 2).transpose(2, 0, 1).reshape(128, 8))

    base = {
        "wq": np.ascontiguousarray(np.asarray(L0_attn_W_q, dtype=f)),
        "wk": np.ascontiguousarray(np.asarray(L0_attn_W_k, dtype=f)),
        "wv": np.ascontiguousarray(np.asarray(L0_attn_W_v, dtype=f)),
        "wo": np.ascontiguousarray(np.asarray(L0_attn_W_o, dtype=f)),
        "gnw": np.ascontiguousarray(np.asarray(L0_attn_gn_weight, dtype=f).reshape(8, 128).T),
        "gnb": np.ascontiguousarray(np.asarray(L0_attn_gn_bias, dtype=f).reshape(8, 128).T),
        "win": np.ascontiguousarray(np.asarray(L0_ffn_Win, dtype=f)),
        "binp": np.ascontiguousarray(np.asarray(L0_ffn_bin_, dtype=f).reshape(64, 1)),
        "w1p": pair(W1), "b1p": pair(b1), "w2p": pair(W2),
        "b2": np.ascontiguousarray(np.asarray(L0_ffn_b2, dtype=f).reshape(64, 1)),
        "wout": np.ascontiguousarray(np.asarray(L0_ffn_Wout, dtype=f)),
        "bout": np.ascontiguousarray(np.asarray(L0_ffn_bout, dtype=f).reshape(1, D)),
        "wmod": np.ascontiguousarray(np.asarray(L0_wmod, dtype=f)),
    }
    in_maps = []
    for b in range(8):
        m = dict(base)
        m["x"] = np.ascontiguousarray(x[b])
        m["crow"] = np.ascontiguousarray(cmat[b].reshape(8, 128).T)
        in_maps.append(m)

    res = run_bass_kernel_spmd(nc, in_maps, core_ids=list(range(8)))
    return np.stack([r["out"] for r in res.results], axis=0)
